# version 02
# baseline (speedup 1.0000x reference)
from contextlib import ExitStack

import numpy as np
import ml_dtypes

import concourse.bass as bass
import concourse.mybir as mybir
from concourse.bass_utils import run_bass_kernel_spmd

F32 = mybir.dt.float32
BF16 = mybir.dt.bfloat16
U8 = mybir.dt.uint8
U16 = mybir.dt.uint16
AF = mybir.ActivationFunctionType
ALU = mybir.AluOpType
BF = ml_dtypes.bfloat16

NCORE = 8
T = 2048
DM = 2048
NT = T // 128
ND = DM // 128

ENGINES = ("sync", "tensor", "vector", "scalar", "gpsimd")



class Sem:
    __slots__ = ("h", "n")

    def __init__(self, h):
        self.h = h
        self.n = 0


class Plan:

    def __init__(self, nc, es):
        self.nc = nc
        self.es = es
        self.q = {e: [] for e in ENGINES}
        self._sems = {}

    def sem(self, name) -> Sem:
        if name not in self._sems:
            self._sems[name] = Sem(self.es.enter_context(self.nc.semaphore(name)))
        return self._sems[name]

    def on(self, eng, fn, inc=None, k=1):
        if inc is not None:
            inc.n += k
            self.q[eng].append(lambda E, fn=fn, h=inc.h, k=k: fn(E).then_inc(h, k))
        else:
            self.q[eng].append(fn)

    def wait(self, eng, sem, val=None):
        v = sem.n if val is None else val
        if v > 0:
            self.q[eng].append(lambda E, h=sem.h, v=v: E.wait_ge(h, v))

    def dma(self, eng, sem, out, in_, **kw):
        sem.n += 16
        self.q[eng].append(
            lambda E, out=out, in_=in_, kw=kw, h=sem.h: E.dma_start(out, in_, **kw).then_inc(h, 16))

    def emit(self):
        block = self.es.enter_context(self.nc.Block())
        for eng in ENGINES:
            fns = self.q[eng]
            if not fns:
                continue

            def make(fns):
                def run(E):
                    for f in fns:
                        f(E)
                return run
            getattr(block, eng)(make(fns))
        self.q = {e: [] for e in ENGINES}



def ln_apply(p, st, pref, yrow, x2row, scratch, cols, epscol):
    vv = p.sem("vv")
    aa = p.sem("aa")
    sc = cols[:, 0:1]
    qc = cols[:, 1:2]
    mc = cols[:, 2:3]
    vc = cols[:, 3:4]
    rc = cols[:, 4:5]
    bc = cols[:, 5:6]
    def vsync():
        p.wait("vector", vv)

    p.on("vector", lambda E: E.tensor_scalar(scratch[:], yrow[:], 1.0, 0.0,
                                             op0=ALU.mult, op1=ALU.add,
                                             accum_out=sc), inc=vv)
    p.on("vector", lambda E: E.scalar_tensor_tensor(scratch[:], yrow[:], 1.0, yrow[:],
                                                    op0=ALU.mult, op1=ALU.mult,
                                                    accum_out=qc), inc=vv)
    vsync()
    p.on("vector", lambda E: E.tensor_scalar(mc, sc, 1.0 / DM, None, op0=ALU.mult), inc=vv)
    vsync()
    p.on("vector", lambda E: E.tensor_scalar(vc, qc, 1.0 / DM, None, op0=ALU.mult), inc=vv)
    vsync()
    p.on("vector", lambda E: E.scalar_tensor_tensor(bc, mc, -1.0, mc, op0=ALU.mult,
                                                    op1=ALU.mult), inc=vv)
    vsync()
    p.on("vector", lambda E: E.tensor_tensor(vc, vc, bc, op=ALU.add), inc=vv)
    p.wait("scalar", vv)
    p.on("scalar", lambda E: E.activation(rc, vc, AF.Ln, bias=epscol[:]), inc=aa)
    p.wait("scalar", aa)
    p.on("scalar", lambda E: E.activation(rc, rc, AF.Exp, scale=-0.5), inc=aa)
    p.wait("vector", aa)
    p.on("vector", lambda E: E.scalar_tensor_tensor(bc, mc, -1.0, rc,
                                                    op0=ALU.mult, op1=ALU.mult), inc=vv)
    p.wait("scalar", vv)
    p.on("scalar", lambda E: E.activation(x2row[:], yrow[:], AF.Identity,
                                          bias=bc, scale=rc), inc=aa)


def pretile_lhsT(wT, kdim=128, mdim=128):
    K, M = wT.shape
    nk, nm = K // kdim, M // mdim
    H = wT.reshape(nk, kdim, nm, mdim).transpose(2, 1, 0, 3).reshape(nm, kdim, nk * mdim)
    return np.ascontiguousarray(H)



def emit_l0f(p: Plan, dr):
    nc, es = p.nc, p.es
    NH = 64
    HID = 8192

    ld = p.sem("ld")
    hw = p.sem("hw")
    mm = p.sem("mm")
    vv = p.sem("vv")
    aa = p.sem("aa")

    with ExitStack() as st:
        e = st.enter_context
        idn = e(nc.sbuf_tensor("f_idn", [128, 128], BF16))
        epscol = e(nc.sbuf_tensor("f_eps", [128, 1], F32))
        s01 = st.enter_context(ExitStack())
        x1T = s01.enter_context(nc.sbuf_tensor("f_x1T", [128, NT * T], BF16))

        p.dma("sync", hw, idn[:], dr["ident"][:, :])
        p.on("vector", lambda E: E.memset(epscol[:], 1e-5), inc=vv)

        with ExitStack() as s0:
            e0 = s0.enter_context
            xrow = [e0(nc.sbuf_tensor(f"f_xr{i}", [128, DM], BF16)) for i in range(2)]
            xrowf = [e0(nc.sbuf_tensor(f"f_xrf{i}", [128, DM], F32)) for i in range(2)]
            wdsl = [e0(nc.sbuf_tensor(f"f_wd{i}", [128, HID], BF16)) for i in range(2)]
            wdslf = e0(nc.sbuf_tensor("f_wdf", [128, HID], F32))
            ps_t = [e0(nc.psum_tensor(f"f_pst{i}", [128, 512], F32)) for i in range(2)]

            p.wait("tensor", hw)
            evac_done = {}
            cvt0 = {}
            for tt in range(NT):
                b = xrow[tt % 2]
                bf_ = xrowf[tt % 2]
                if tt >= 2:
                    p.wait("sync", aa, cvt0[tt - 2])
                p.dma("sync", hw, bf_[:], dr["x1s"][tt * 128:(tt + 1) * 128, :])
                p.wait("scalar", hw)
                if tt >= 2:
                    p.wait("scalar", mm)
                p.on("scalar", lambda E, b=b, bf_=bf_: E.activation(
                    b[:], bf_[:], AF.Identity), inc=aa)
                cvt0[tt] = aa.n
                p.wait("tensor", aa, cvt0[tt])
                x1Tv = x1T[:].rearrange("p (dt tc) -> p dt tc", dt=ND)
                for g in range(4):
                    bank = ps_t[g % 2]
                    if tt * 4 + g >= 2:
                        p.wait("tensor", vv, vv.n - 1)
                    for j in range(4):
                        dt = 4 * g + j
                        p.on("tensor",
                             lambda E, b=b, dt=dt, j=j, bank=bank: E.matmul(
                                 bank[:, j * 128:j * 128 + 128],
                                 b[:, dt * 128:dt * 128 + 128], idn[:], start=True, stop=True),
                             inc=mm)
                    p.wait("vector", mm)
                    p.on("vector",
                         lambda E, g=g, tt=tt, bank=bank, x1Tv=x1Tv: E.tensor_copy(
                             x1Tv[:, 4 * g:4 * g + 4, tt * 128:tt * 128 + 128],
                             bank[:, 0:512].rearrange("p (j tc) -> p j tc", j=4)),
                         inc=vv)
                evac_done[tt] = vv.n
            x1T_done = vv.n
            p.wait("sync", vv, x1T_done)
            for dt in range(ND):
                p.dma("sync", hw, dr["x1Ts"][dt, :, :], x1T[:, dt * T:(dt + 1) * T])
            x1Ts_stored = hw.n

            wd_store = {}
            for dt in range(ND):
                b = wdsl[dt % 2]
                p.dma("sync", hw, wdslf[:], dr["wd"][dt, :, :])
                p.wait("scalar", hw)
                if dt >= 2:
                    p.wait("scalar", hw, wd_store[dt - 2])
                p.on("scalar", lambda E, b=b: E.activation(b[:], wdslf[:], AF.Identity), inc=aa)
                p.wait("sync", aa)
                p.dma("sync", hw, dr["wdbf"][dt, :, :], b[:])
                wd_store[dt] = hw.n
            wdbf_done = hw.n

        with ExitStack() as s1_:
            e1 = s1_.enter_context
            wsl = [e1(nc.sbuf_tensor(f"f_w{i}", [128, DM], BF16)) for i in range(4)]
            wslf = [e1(nc.sbuf_tensor(f"f_wlf{i}", [128, DM], F32)) for i in range(2)]
            s1b = e1(nc.sbuf_tensor("f_s1", [128, T], BF16))
            hbuf = [e1(nc.sbuf_tensor(f"f_h{i}", [128, T], BF16)) for i in range(2)]
            ps = [e1(nc.psum_tensor(f"f_ps{i}", [128, 512], F32)) for i in range(8)]

            mm_done = {}
            ev_a = {}
            ev_v = {}
            hst = {}
            p.wait("tensor", vv, x1T_done)
            p.wait("gpsimd", mm)
            p.wait("gpsimd", hw)
            for ht in range(NH):
                g = wsl[2 * (ht % 2)]
                u = wsl[2 * (ht % 2) + 1]
                if ht >= 1:
                    p.wait("sync", aa, mm_done.get(("cast", ht - 1), 0))
                p.dma("sync", hw, wslf[0][:], dr["wg"][ht, :, :])
                p.dma("sync", hw, wslf[1][:], dr["wu"][ht, :, :])
                p.wait("scalar", hw)
                if ht >= 2:
                    p.wait("scalar", mm, mm_done[(ht - 2, 1)])
                p.on("scalar", lambda E, g=g: E.activation(g[:], wslf[0][:], AF.Identity), inc=aa)
                p.on("scalar", lambda E, u=u: E.activation(u[:], wslf[1][:], AF.Identity), inc=aa)
                mm_done[("cast", ht)] = aa.n
                wready = aa.n
                for th in range(2):
                    bank = (th % 2) * 4
                    p.wait("tensor", aa, wready)
                    if (ht - 1, th) in ev_a:
                        p.wait("tensor", aa, ev_a[(ht - 1, th)])
                        p.wait("tensor", vv, ev_v[(ht - 1, th)])
                    for wi, w in ((0, g), (1, u)):
                        for dt in range(ND):
                            for tb in range(2):
                                p.on("tensor",
                                     lambda E, w=w, wi=wi, dt=dt, th=th, tb=tb, bank=bank: E.matmul(
                                         ps[bank + 2 * wi + tb][:],
                                         w[:, dt * 128:dt * 128 + 128],
                                         x1T[:, dt * T + th * 1024 + tb * 512:
                                             dt * T + th * 1024 + tb * 512 + 512],
                                         start=(dt == 0), stop=(dt == ND - 1)),
                                     inc=(mm if (dt == ND - 1 and tb == 1) else None))
                    mm_done[(ht, th)] = mm.n
                    hb = hbuf[ht % 2]
                    p.wait("scalar", mm, mm_done[(ht, th)])
                    if (ht - 1, th) in ev_v:
                        p.wait("scalar", vv, ev_v[(ht - 1, th)])
                    for tb in range(2):
                        p.on("scalar",
                             lambda E, th=th, tb=tb, bank=bank: E.activation(
                                 s1b[:, th * 1024 + tb * 512: th * 1024 + tb * 512 + 512],
                                 ps[bank + tb][:], AF.Silu),
                             inc=aa)
                    ev_a[(ht, th)] = aa.n
                    p.wait("vector", aa, ev_a[(ht, th)])
                    if ht >= 2 and th == 0:
                        p.wait("vector", hw, hst[ht - 2])
                    for tb in range(2):
                        p.on("vector",
                             lambda E, hb=hb, th=th, tb=tb, bank=bank: E.tensor_tensor(
                                 hb[:, th * 1024 + tb * 512: th * 1024 + tb * 512 + 512],
                                 s1b[:, th * 1024 + tb * 512: th * 1024 + tb * 512 + 512],
                                 ps[bank + 2 + tb][:], op=ALU.mult),
                             inc=vv)
                    ev_v[(ht, th)] = vv.n
                p.wait("sync", vv, ev_v[(ht, 1)])
                p.dma("sync", hw, dr["hs"][ht, :, :], hbuf[ht % 2][:])
                hst[ht] = hw.n
            h_all = hw.n

        s01.close()

        with ExitStack() as s2:
            e2 = s2.enter_context
            hhalf = e2(nc.sbuf_tensor("f_hh", [128, NH * 1024], BF16))
            wdsl2 = [e2(nc.sbuf_tensor(f"f_wd2{i}", [128, HID], BF16)) for i in range(2)]
            xsl = [e2(nc.sbuf_tensor(f"f_xs{i}", [128, 1024], BF16)) for i in range(2)]
            yb = [e2(nc.sbuf_tensor(f"f_yb{i}", [128, 1024], BF16)) for i in range(2)]
            psy = [e2(nc.psum_tensor(f"f_py{i}", [128, 512], F32)) for i in range(4)]

            p.wait("sync", hw, h_all)
            p.wait("gpsimd", hw, max(wdbf_done, x1Ts_stored))
            ymm = {}
            yev = {}
            yst = {}
            for th in range(2):
                if th == 1:
                    p.wait("sync", mm, ymm[(0, ND - 1)])
                for ht in range(NH):
                    p.dma("sync", hw, hhalf[:, ht * 1024:(ht + 1) * 1024],
                          dr["hs"][ht, :, th * 1024:th * 1024 + 1024])
                hread = hw.n
                for dt in range(ND):
                    w2 = wdsl2[dt % 2]
                    if (th, dt) not in ((0, 0), (0, 1)):
                        pdt = (th, dt - 2) if dt >= 2 else (th - 1, ND - 2 + dt)
                        p.wait("gpsimd", vv, yev[pdt])
                    p.dma("gpsimd", ld, w2[:], dr["wdbf"][dt, :, :])
                    p.wait("tensor", ld)
                    p.wait("tensor", hw, hread)
                    bank = (dt % 2) * 2
                    if (th, dt) not in ((0, 0), (0, 1)):
                        pdt = (th, dt - 2) if dt >= 2 else (th - 1, ND - 2 + dt)
                        p.wait("tensor", vv, yev[pdt])
                    for ht in range(NH):
                        for tb in range(2):
                            p.on("tensor",
                                 lambda E, w2=w2, ht=ht, tb=tb, bank=bank: E.matmul(
                                     psy[bank + tb][:],
                                     w2[:, ht * 128:ht * 128 + 128],
                                     hhalf[:, ht * 1024 + tb * 512: ht * 1024 + tb * 512 + 512],
                                     start=(ht == 0), stop=(ht == NH - 1)),
                                 inc=(mm if (ht == NH - 1 and tb == 1) else None))
                    ymm[(th, dt)] = mm.n
                    b = yb[dt % 2]
                    p.wait("vector", mm, ymm[(th, dt)])
                    if (th, dt) not in ((0, 0), (0, 1)):
                        pdt = (th, dt - 2) if dt >= 2 else (th - 1, ND - 2 + dt)
                        p.wait("vector", hw, yst[pdt])
                    for tb in range(2):
                        p.on("vector",
                             lambda E, b=b, tb=tb, bank=bank: E.tensor_copy(
                                 b[:, tb * 512:tb * 512 + 512], psy[bank + tb][:]),
                             inc=vv)
                    yev[(th, dt)] = vv.n
                    p.wait("sync", vv, yev[(th, dt)])
                    p.dma("sync", hw, dr["ys"][dt, :, th * 1024:th * 1024 + 1024], b[:])
                    yst[(th, dt)] = hw.n
            y_all = hw.n

        with ExitStack() as s3:
            e3 = s3.enter_context
            ybig = [e3(nc.sbuf_tensor(f"f_ybig{i}", [128, DM], BF16)) for i in range(2)]
            yrow = [e3(nc.sbuf_tensor(f"f_yrow{i}", [128, DM], BF16)) for i in range(2)]
            x1fr = [e3(nc.sbuf_tensor(f"f_x1f{i}", [128, DM], F32)) for i in range(2)]
            scr = e3(nc.sbuf_tensor("f_scr", [128, DM], BF16))
            cols = e3(nc.sbuf_tensor("f_cols", [128, 8], F32))
            x2row = e3(nc.sbuf_tensor("f_x2r", [128, DM], F32))
            ps_t2 = [e3(nc.psum_tensor(f"f_pt2{i}", [128, 512], F32)) for i in range(2)]

            p.wait("sync", hw, y_all)
            apply_done = {}
            for tt in range(NT):
                b = ybig[tt % 2]
                r = yrow[tt % 2]
                if tt >= 2:
                    p.wait("sync", mm)
                p.dma("sync", hw, b[:].rearrange("p (dt f) -> p dt f", dt=ND),
                      dr["ys"][:, :, tt * 128:tt * 128 + 128].rearrange("dt p f -> p dt f"))
                p.dma("sync", hw, x1fr[tt % 2][:], dr["x1s"][tt * 128:(tt + 1) * 128, :])
                p.wait("tensor", hw)
                if tt >= 2:
                    p.wait("vector", aa, apply_done.get(tt - 2, 0))
                for g in range(4):
                    bank = ps_t2[g % 2]
                    if tt * 4 + g >= 2:
                        p.wait("tensor", vv, vv.n - 1)
                    for j in range(4):
                        dt = 4 * g + j
                        p.on("tensor",
                             lambda E, b=b, dt=dt, j=j, bank=bank: E.matmul(
                                 bank[:, j * 128:j * 128 + 128],
                                 b[:, dt * 128:dt * 128 + 128], idn[:], start=True, stop=True),
                             inc=mm)
                    p.wait("vector", mm)
                    p.on("vector",
                         lambda E, r=r, g=g, bank=bank: E.tensor_copy(
                             r[:, 4 * g * 128:4 * g * 128 + 512],
                             bank[:, 0:512]),
                         inc=vv)
                p.wait("vector", hw)
                p.on("vector", lambda E, r=r, tt=tt: E.scalar_tensor_tensor(
                    r[:], r[:], 1.0, x1fr[tt % 2][:], op0=ALU.mult, op1=ALU.add),
                    inc=p.sem("vv"))
                p.wait("scalar", hw)
                ln_apply(p, s3, f"f{tt}", r, x2row, scr, cols, epscol)
                apply_done[tt] = aa.n
                p.wait("sync", aa)
                p.dma("sync", hw, dr["x2s"][tt * 128:(tt + 1) * 128, :], x2row[:])
            p.wait("sync", hw)
            p.wait("gpsimd", hw)




def emit_l1a(p: Plan, dr, stop_after=None):
    nc, es = p.nc, p.es
    NG, HPG = 4, 4
    QW, GW = HPG * 64, HPG * 128
    PW = QW + GW

    ld = p.sem("ld")
    hw = p.sem("hw")
    mm = p.sem("mm")
    vv = p.sem("vv")
    aa = p.sem("aa")

    def vs():
        p.wait("vector", vv)

    def asy():
        p.wait("scalar", aa)

    with ExitStack() as st:
        e = st.enter_context
        idn = e(nc.sbuf_tensor("a_idn", [128, 128], BF16))
        epscol = e(nc.sbuf_tensor("a_eps", [128, 1], F32))
        rbc = e(nc.sbuf_tensor("a_rbc", [128, 16], F32))
        obc = e(nc.sbuf_tensor("a_obc", [128, 16], F32))
        ebc = e(nc.sbuf_tensor("a_ebc", [128, 16], F32))
        x2T = e(nc.sbuf_tensor("a_x2T", [128, NT * T], BF16))

        lsal = e(nc.sbuf_tensor("a_lsal", [128, 64], F32))
        p.dma("sync", hw, idn[:], dr["ident"][:, :])
        p.dma("gpsimd", ld, lsal[:], dr["lsal"][:, :])
        p.on("vector", lambda E: E.memset(epscol[:], 1e-5), inc=vv)

        p.wait("scalar", ld)
        p.on("scalar", lambda E: E.activation(rbc[:], lsal[:, 0:16], AF.Exp), inc=aa)
        p.wait("scalar", aa)
        p.on("scalar", lambda E: E.activation(ebc[:], lsal[:, 16:32], AF.Exp), inc=aa)
        p.wait("vector", aa)
        p.on("vector", lambda E: E.tensor_scalar(rbc[:], rbc[:],
                                                 1.0, None, op0=ALU.add), inc=vv)
        vs()
        p.on("vector", lambda E: E.reciprocal(rbc[:], rbc[:]), inc=vv)
        vs()
        p.on("vector", lambda E: E.tensor_scalar(obc[:], rbc[:],
                                                 -1.0, 1.0, op0=ALU.mult, op1=ALU.add), inc=vv)
        vs()

        with ExitStack() as s0:
            e0 = s0.enter_context
            xrow = [e0(nc.sbuf_tensor(f"a_xr{i}", [128, DM], BF16)) for i in range(2)]
            xrowf = [e0(nc.sbuf_tensor(f"a_xrf{i}", [128, DM], F32)) for i in range(2)]
            ps_t = [e0(nc.psum_tensor(f"a_pst{i}", [128, 512], F32)) for i in range(2)]
            p.wait("tensor", hw)
            p.wait("tensor", vv)
            x2Tv = x2T[:].rearrange("p (dt tc) -> p dt tc", dt=ND)
            evd = {}
            cvt0 = {}
            for tt in range(NT):
                b = xrow[tt % 2]
                bf_ = xrowf[tt % 2]
                if tt >= 2:
                    p.wait("sync", aa, cvt0[tt - 2])
                p.dma("sync", hw, bf_[:], dr["x2s"][tt * 128:(tt + 1) * 128, :])
                p.wait("scalar", hw)
                if tt >= 2:
                    p.wait("scalar", mm)
                p.on("scalar", lambda E, b=b, bf_=bf_: E.activation(
                    b[:], bf_[:], AF.Identity), inc=aa)
                cvt0[tt] = aa.n
                p.wait("tensor", aa, cvt0[tt])
                for g4 in range(4):
                    bank = ps_t[g4 % 2]
                    if tt * 4 + g4 >= 2:
                        p.wait("tensor", vv, vv.n - 1)
                    for j in range(4):
                        dt = 4 * g4 + j
                        p.on("tensor",
                             lambda E, b=b, dt=dt, j=j, bank=bank: E.matmul(
                                 bank[:, j * 128:j * 128 + 128],
                                 b[:, dt * 128:dt * 128 + 128], idn[:], start=True, stop=True),
                             inc=mm)
                    p.wait("vector", mm)
                    p.on("vector",
                         lambda E, g4=g4, tt=tt, bank=bank: E.tensor_copy(
                             x2Tv[:, 4 * g4:4 * g4 + 4, tt * 128:tt * 128 + 128],
                             bank[:, 0:512].rearrange("p (j tc) -> p j tc", j=4)),
                         inc=vv)
                evd[tt] = vv.n
        x2T_done = vv.n
        if stop_after == "x2T":
            p.wait("sync", vv, x2T_done)
            for dt in range(ND):
                p.dma("sync", hw, dr["x3s"][dt * 128:(dt + 1) * 128, :],
                      x2T[:, dt * T:(dt + 1) * T])
            p.wait("sync", hw)
            return

        for g in range(NG):
            with ExitStack() as sg:
                eg = sg.enter_context
                qg_t = eg(nc.sbuf_tensor(f"a_qg{g}", [128, NT * PW], BF16))
                kv_t = eg(nc.sbuf_tensor(f"a_kv{g}", [128, NT * PW], BF16))
                ncols = eg(nc.sbuf_tensor(f"a_nc{g}", [128, NT * 12], F32))
                scr = eg(nc.sbuf_tensor(f"a_scr{g}", [128, PW], BF16))
                qhc = eg(nc.sbuf_tensor(f"a_qhc{g}", [128, 2 * T], BF16))
                khc = eg(nc.sbuf_tensor(f"a_khc{g}", [128, 2 * T], BF16))

                if g >= 1:
                    for engx in ("gpsimd", "vector", "scalar", "tensor", "sync"):
                        p.wait(engx, hw)
                        p.wait(engx, mm)
                        p.wait(engx, vv)
                        p.wait(engx, aa)

                with ExitStack() as sp:
                    ep = sp.enter_context
                    wq_s = ep(nc.sbuf_tensor(f"a_wqs{g}", [128, ND * PW], BF16))
                    wk_s = ep(nc.sbuf_tensor(f"a_wks{g}", [128, ND * PW], BF16))
                    wgf = ep(nc.sbuf_tensor(f"a_wgf{g}", [128, PW], F32))
                    pq = [ep(nc.psum_tensor(f"a_pq{g}_{i}", [128, 512], F32)) for i in range(4)]
                    pk = [ep(nc.psum_tensor(f"a_pk{g}_{i}", [128, 512], F32)) for i in range(4)]
                    for dt in range(ND):
                        p.dma("sync", hw, wgf[:], dr["wqg"][g, dt, :, :])
                        p.wait("scalar", hw)
                        p.on("scalar", lambda E, dt=dt: E.activation(
                            wq_s[:, dt * PW:(dt + 1) * PW], wgf[:], AF.Identity), inc=aa)
                        p.wait("sync", aa)
                        p.dma("sync", hw, wgf[:], dr["wkv"][g, dt, :, :])
                        p.wait("scalar", hw)
                        p.on("scalar", lambda E, dt=dt: E.activation(
                            wk_s[:, dt * PW:(dt + 1) * PW], wgf[:], AF.Identity), inc=aa)
                        p.wait("sync", aa)
                    p.wait("tensor", aa)
                    p.wait("tensor", vv, x2T_done)
                    prev_ev = {}
                    for tt in range(NT):
                        pb = tt % 2
                        if tt >= 2:
                            p.wait("tensor", vv, prev_ev[tt - 2])
                        for dt in range(ND):
                            xsl = x2T[:, dt * T + tt * 128: dt * T + tt * 128 + 128]
                            st_ = (dt == 0)
                            sp_ = (dt == ND - 1)
                            p.on("tensor", lambda E, xsl=xsl, dt=dt, pb=pb, st_=st_, sp_=sp_:
                                 E.matmul(pq[2 * pb][:], xsl,
                                          wq_s[:, dt * PW: dt * PW + 512],
                                          start=st_, stop=sp_))
                            p.on("tensor", lambda E, xsl=xsl, dt=dt, pb=pb, st_=st_, sp_=sp_:
                                 E.matmul(pq[2 * pb + 1][:, 0:256], xsl,
                                          wq_s[:, dt * PW + 512: dt * PW + PW],
                                          start=st_, stop=sp_))
                            p.on("tensor", lambda E, xsl=xsl, dt=dt, pb=pb, st_=st_, sp_=sp_:
                                 E.matmul(pk[2 * pb][:], xsl,
                                          wk_s[:, dt * PW: dt * PW + 512],
                                          start=st_, stop=sp_))
                            p.on("tensor", lambda E, xsl=xsl, dt=dt, pb=pb, st_=st_, sp_=sp_:
                                 E.matmul(pk[2 * pb + 1][:, 0:256], xsl,
                                          wk_s[:, dt * PW + 512: dt * PW + PW],
                                          start=st_, stop=sp_),
                                 inc=(mm if sp_ else None))
                        p.wait("vector", mm)
                        p.on("vector", lambda E, tt=tt, pb=pb: E.tensor_copy(
                            qg_t[:, tt * PW: tt * PW + 512], pq[2 * pb][:]), inc=vv)
                        p.on("vector", lambda E, tt=tt, pb=pb: E.tensor_copy(
                            qg_t[:, tt * PW + 512: tt * PW + PW], pq[2 * pb + 1][:, 0:256]), inc=vv)
                        p.on("vector", lambda E, tt=tt, pb=pb: E.tensor_copy(
                            kv_t[:, tt * PW: tt * PW + 512], pk[2 * pb][:]), inc=vv)
                        p.on("vector", lambda E, tt=tt, pb=pb: E.tensor_copy(
                            kv_t[:, tt * PW + 512: tt * PW + PW], pk[2 * pb + 1][:, 0:256]), inc=vv)
                        prev_ev[tt] = vv.n

                for tt in range(NT):
                    for hh in range(HPG):
                        q0 = tt * PW + hh * 64
                        v0 = tt * PW + QW + hh * 128
                        c0 = tt * 12
                        p.on("vector", lambda E, q0=q0, c0=c0, hh=hh: E.scalar_tensor_tensor(
                            scr[:, 0:64], qg_t[:, q0:q0 + 64], 1.0, qg_t[:, q0:q0 + 64],
                            op0=ALU.mult, op1=ALU.mult,
                            accum_out=ncols[:, c0 + hh: c0 + hh + 1]), inc=vv)
                        p.on("vector", lambda E, q0=q0, c0=c0, hh=hh: E.scalar_tensor_tensor(
                            scr[:, 64:128], kv_t[:, q0:q0 + 64], 1.0, kv_t[:, q0:q0 + 64],
                            op0=ALU.mult, op1=ALU.mult,
                            accum_out=ncols[:, c0 + 4 + hh: c0 + 4 + hh + 1]), inc=vv)
                        p.on("vector", lambda E, v0=v0, c0=c0, hh=hh: E.scalar_tensor_tensor(
                            scr[:, 128:256], kv_t[:, v0:v0 + 128], 1.0, kv_t[:, v0:v0 + 128],
                            op0=ALU.mult, op1=ALU.mult,
                            accum_out=ncols[:, c0 + 8 + hh: c0 + 8 + hh + 1]), inc=vv)
                vs()
                p.on("vector", lambda E: E.tensor_scalar(ncols[:], ncols[:], 1e-24, None,
                                                         op0=ALU.max), inc=vv)
                p.wait("scalar", vv)
                p.on("scalar", lambda E: E.activation(ncols[:], ncols[:], AF.Ln), inc=aa)
                asy()
                p.on("scalar", lambda E: E.activation(ncols[:], ncols[:], AF.Exp, scale=-0.5), inc=aa)
                p.wait("vector", aa)
                for tt in range(NT):
                    for hh in range(HPG):
                        q0 = tt * PW + hh * 64
                        v0 = tt * PW + QW + hh * 128
                        c0 = tt * 12
                        p.on("vector", lambda E, q0=q0, c0=c0, hh=hh: E.tensor_scalar(
                            qg_t[:, q0:q0 + 64], qg_t[:, q0:q0 + 64],
                            ncols[:, c0 + hh: c0 + hh + 1], None, op0=ALU.mult), inc=vv)
                        p.on("vector", lambda E, q0=q0, c0=c0, hh=hh: E.tensor_scalar(
                            kv_t[:, q0:q0 + 64], kv_t[:, q0:q0 + 64],
                            ncols[:, c0 + 4 + hh: c0 + 4 + hh + 1], None, op0=ALU.mult), inc=vv)
                        p.on("vector", lambda E, v0=v0, c0=c0, hh=hh: E.tensor_scalar(
                            kv_t[:, v0:v0 + 128], kv_t[:, v0:v0 + 128],
                            ncols[:, c0 + 8 + hh: c0 + 8 + hh + 1], None, op0=ALU.mult), inc=vv)
                norm_done = vv.n
                if stop_after == "norm":
                    p.wait("sync", vv, norm_done)
                    for tt in range(NT):
                        p.dma("sync", hw, dr["x3s"][tt * 128:(tt + 1) * 128, 0:768],
                              qg_t[:, tt * PW:(tt + 1) * PW])
                        p.dma("sync", hw, dr["x3s"][tt * 128:(tt + 1) * 128, 1024:1792],
                              kv_t[:, tt * PW:(tt + 1) * PW])
                    p.wait("sync", hw)
                    return
                p.wait("scalar", vv, norm_done)
                for tt in range(NT):
                    p.on("scalar", lambda E, tt=tt: E.activation(
                        qg_t[:, tt * PW + QW: (tt + 1) * PW],
                        qg_t[:, tt * PW + QW: (tt + 1) * PW], AF.Silu), inc=aa)
                asy()
                for tt in range(NT):
                    p.on("scalar", lambda E, tt=tt: E.activation(
                        qg_t[:, tt * PW + QW: (tt + 1) * PW],
                        qg_t[:, tt * PW + QW: (tt + 1) * PW], AF.Tanh), inc=aa)
                gates_done = aa.n

                with ExitStack() as stp:
                    pst = [stp.enter_context(nc.psum_tensor(f"a_pt2{g}_{i}", [128, 512], F32))
                           for i in range(2)]
                    p.wait("tensor", vv, norm_done)
                    qv = qhc[:].rearrange("p (pc tc) -> p pc tc", pc=2)
                    kv2 = khc[:].rearrange("p (pc tc) -> p pc tc", pc=2)
                    trv = {}
                    for tt in range(NT):
                        bank = pst[tt % 2]
                        if tt >= 2:
                            p.wait("tensor", vv, trv[tt - 2])
                        for j in range(2):
                            p.on("tensor", lambda E, tt=tt, j=j, bank=bank: E.matmul(
                                bank[:, j * 128:j * 128 + 128],
                                qg_t[:, tt * PW + j * 128: tt * PW + (j + 1) * 128], idn[:],
                                start=True, stop=True))
                        for j in range(2):
                            p.on("tensor", lambda E, tt=tt, j=j, bank=bank: E.matmul(
                                bank[:, 256 + j * 128: 256 + (j + 1) * 128],
                                kv_t[:, tt * PW + j * 128: tt * PW + (j + 1) * 128], idn[:], start=True, stop=True),
                                inc=mm)
                        p.wait("vector", mm)
                        p.on("vector", lambda E, tt=tt, bank=bank, qv=qv: E.tensor_copy(
                            qv[:, :, tt * 128:tt * 128 + 128],
                            bank[:, 0:256].rearrange("p (pc f) -> p pc f", pc=2)))
                        p.on("vector", lambda E, tt=tt, bank=bank, kv2=kv2: E.tensor_copy(
                            kv2[:, :, tt * 128:tt * 128 + 128],
                            bank[:, 256:512].rearrange("p (pc f) -> p pc f", pc=2)), inc=vv)
                        trv[tt] = vv.n

                with ExitStack() as sh:
                    eh = sh.enter_context
                    att = [eh(nc.sbuf_tensor(f"a_att{g}_{i}", [128, T], BF16)) for i in range(2)]
                    ag_sb = eh(nc.sbuf_tensor(f"a_agsb{g}", [128, T], BF16))
                    ecol = eh(nc.sbuf_tensor(f"a_ecol{g}", [128, 64], F32))
                    aggO = eh(nc.sbuf_tensor(f"a_aggO{g}", [128, NT * GW], BF16))
                    pss = [eh(nc.psum_tensor(f"a_ps{g}_{i}", [128, 512], F32)) for i in range(4)]
                    psa = [eh(nc.psum_tensor(f"a_pa{g}_{i}", [128, 512], F32)) for i in range(4)]
                    p.wait("tensor", vv)
                    p.wait("tensor", aa, gates_done)
                    prev_relu = 0
                    prev_epi = 0
                    agg_done = {}
                    sq_done = {}
                    for hh in range(HPG):
                        h = g * HPG + hh
                        rb = (hh % 2) * 64
                        pc = hh // 2
                        for jt in range(NT):
                            ab = att[jt % 2]
                            p.wait("tensor", aa, prev_relu)
                            if jt >= 2:
                                p.wait("tensor", vv, sq_done[(hh, jt - 2)])
                            if jt == 0 and hh > 0:
                                p.wait("tensor", vv, prev_epi)
                            for ib in range(4):
                                p.on("tensor", lambda E, rb=rb, pc=pc, jt=jt, ib=ib: E.matmul(
                                    pss[ib][:],
                                    khc[rb:rb + 64, pc * T + jt * 128: pc * T + jt * 128 + 128],
                                    qhc[rb:rb + 64, pc * T + ib * 512: pc * T + (ib + 1) * 512],
                                    start=True, stop=True),
                                    inc=(mm if ib == 3 else None))
                            smm = mm.n
                            p.wait("scalar", mm, smm)
                            if jt >= 2:
                                p.wait("scalar", mm, agg_done[(hh, jt - 2)])
                            if jt <= 1 and hh > 0:
                                p.wait("scalar", mm, agg_done[(hh - 1, NT - 2 + jt)])
                            for ib in range(4):
                                p.on("scalar", lambda E, ab=ab, ib=ib, h=h: E.activation(
                                    ab[:, ib * 512:(ib + 1) * 512], pss[ib][:], AF.Relu,
                                    bias=obc[:, h:h + 1], scale=rbc[:, h:h + 1]), inc=aa)
                            prev_relu = aa.n
                            p.wait("vector", aa, prev_relu)
                            p.on("vector", lambda E, ab=ab: E.tensor_tensor(
                                ab[:], ab[:], ab[:], op=ALU.mult), inc=vv)
                            sq_done[(hh, jt)] = vv.n
                            p.wait("tensor", vv, sq_done[(hh, jt)])
                            for it in range(NT):
                                p.on("tensor", lambda E, ab=ab, it=it, jt=jt, hh=hh: E.matmul(
                                    psa[it // 4][:, (it % 4) * 128:(it % 4) * 128 + 128],
                                    ab[:, it * 128:it * 128 + 128],
                                    kv_t[:, jt * PW + QW + hh * 128: jt * PW + QW + (hh + 1) * 128],
                                    start=(jt == 0), stop=(jt == NT - 1)),
                                    inc=(mm if it == NT - 1 else None))
                            agg_done[(hh, jt)] = mm.n
                        p.wait("vector", mm, agg_done[(hh, NT - 1)])
                        for it in range(NT):
                            p.on("vector", lambda E, it=it: E.tensor_copy(
                                ag_sb[:, it * 128:(it + 1) * 128],
                                psa[it // 4][:, (it % 4) * 128:(it % 4) * 128 + 128]), inc=vv)
                        prev_epi = vv.n
                        vs()
                        for it in range(NT):
                            p.on("vector", lambda E, it=it: E.scalar_tensor_tensor(
                                scr[:, 0:128], ag_sb[:, it * 128:(it + 1) * 128], 1.0,
                                ag_sb[:, it * 128:(it + 1) * 128], op0=ALU.mult, op1=ALU.mult,
                                accum_out=ecol[:, it: it + 1]), inc=vv)
                        vs()
                        p.on("vector", lambda E: E.tensor_scalar(
                            ecol[:, 0:16], ecol[:, 0:16], 1e-24, None, op0=ALU.max), inc=vv)
                        p.wait("scalar", vv)
                        p.on("scalar", lambda E: E.activation(ecol[:, 16:32], ecol[:, 0:16],
                                                              AF.Ln), inc=aa)
                        asy()
                        p.on("scalar", lambda E: E.activation(ecol[:, 32:48], ecol[:, 16:32],
                                                              AF.Exp, scale=-0.5), inc=aa)
                        p.on("scalar", lambda E: E.activation(ecol[:, 48:64], ecol[:, 16:32],
                                                              AF.Exp, scale=0.5), inc=aa)
                        p.wait("vector", aa)
                        p.on("vector", lambda E: E.tensor_scalar(
                            ecol[:, 48:64], ecol[:, 48:64], 20.0, None, op0=ALU.min), inc=vv)
                        p.wait("scalar", vv)
                        p.on("scalar", lambda E: E.activation(ecol[:, 48:64], ecol[:, 48:64],
                                                              AF.Exp, scale=2.0), inc=aa)
                        p.wait("vector", aa)
                        p.on("vector", lambda E: E.tensor_scalar(
                            ecol[:, 48:64], ecol[:, 48:64], 1.0, None, op0=ALU.add), inc=vv)
                        vs()
                        p.on("vector", lambda E: E.reciprocal(ecol[:, 16:32], ecol[:, 48:64]), inc=vv)
                        vs()
                        p.on("vector", lambda E: E.tensor_scalar(
                            ecol[:, 16:32], ecol[:, 16:32], -2.0, 1.0,
                            op0=ALU.mult, op1=ALU.add), inc=vv)
                        vs()
                        p.on("vector", lambda E: E.tensor_tensor(
                            ecol[:, 0:16], ecol[:, 16:32], ecol[:, 32:48], op=ALU.mult), inc=vv)
                        vs()
                        for it in range(NT):
                            p.on("vector", lambda E, it=it, h=h: E.tensor_scalar(
                                ag_sb[:, it * 128:(it + 1) * 128],
                                ag_sb[:, it * 128:(it + 1) * 128],
                                ecol[:, it:it + 1], ebc[:, h:h + 1],
                                op0=ALU.mult, op1=ALU.mult), inc=vv)
                        vs()
                        for it in range(NT):
                            p.on("vector", lambda E, it=it, hh=hh: E.tensor_tensor(
                                aggO[:, it * GW + hh * 128: it * GW + (hh + 1) * 128],
                                ag_sb[:, it * 128:(it + 1) * 128],
                                qg_t[:, it * PW + QW + hh * 128: it * PW + QW + (hh + 1) * 128],
                                op=ALU.mult), inc=vv)
                    p.wait("sync", vv)
                    p.dma("sync", hw, dr["aggOs"][g].rearrange("tt p f -> p tt f"),
                          aggO[:].rearrange("p (tt f) -> p tt f", tt=NT))
                    p.wait("gpsimd", hw)
                    if stop_after == "group0":
                        for tt in range(NT):
                            p.dma("sync", hw, dr["x3s"][tt * 128:(tt + 1) * 128, 0:512],
                                  aggO[:, tt * GW:(tt + 1) * GW])
                        p.wait("sync", hw)
                        return

        with ExitStack() as sw:
            ew = sw.enter_context
            wos = ew(nc.sbuf_tensor("a_wos", [128, ND * DM], BF16))
            wosf = ew(nc.sbuf_tensor("a_wosf", [128, DM], F32))
            arow = [ew(nc.sbuf_tensor(f"a_ar{i}", [128, DM], BF16)) for i in range(2)]
            aot = ew(nc.sbuf_tensor("a_aot", [128, DM], BF16))
            x2r = [ew(nc.sbuf_tensor(f"a_x2r{i}", [128, DM], F32)) for i in range(2)]
            yrow = ew(nc.sbuf_tensor("a_yrow", [128, DM], BF16))
            x3row = ew(nc.sbuf_tensor("a_x3r", [128, DM], F32))
            lnscr = ew(nc.sbuf_tensor("a_lscr", [128, DM], BF16))
            cols = ew(nc.sbuf_tensor("a_cols", [128, 8], F32))
            pstw = [ew(nc.psum_tensor(f"a_ptw{i}", [128, 512], F32)) for i in range(2)]
            pso = [ew(nc.psum_tensor(f"a_po{i}", [128, 512], F32)) for i in range(4)]

            for engx in ("gpsimd", "vector", "scalar", "tensor", "sync"):
                p.wait(engx, hw)
                p.wait(engx, vv)
                p.wait(engx, aa)
                p.wait(engx, mm)
            for vt in range(ND):
                p.dma("sync", hw, wosf[:], dr["wo"][vt * 128:(vt + 1) * 128, :])
                p.wait("scalar", hw)
                p.on("scalar", lambda E, vt=vt: E.activation(
                    wos[:, vt * DM:(vt + 1) * DM], wosf[:], AF.Identity), inc=aa)
                p.wait("sync", aa)
            p.wait("tensor", aa)
            resid_done = {}
            for tt in range(NT):
                ar = arow[tt % 2]
                xr = x2r[tt % 2]
                if tt >= 2:
                    p.wait("sync", mm)
                    p.wait("sync", vv, resid_done[tt - 2])
                p.dma("sync", hw, ar[:].rearrange("p (gg f) -> p gg f", gg=4),
                      dr["aggOs"][:, tt, :, :].rearrange("gg p f -> p gg f"))
                p.dma("sync", hw, xr[:], dr["x2s"][tt * 128:(tt + 1) * 128, :])
                p.wait("tensor", hw)
                if tt >= 1:
                    p.wait("tensor", vv)
                trn = {}
                for g4 in range(4):
                    bank = pstw[g4 % 2]
                    if tt * 4 + g4 >= 2:
                        p.wait("tensor", vv, vv.n - 1)
                    for j in range(4):
                        vt = 4 * g4 + j
                        p.on("tensor", lambda E, ar=ar, vt=vt, j=j, bank=bank: E.matmul(
                            bank[:, j * 128:j * 128 + 128],
                            ar[:, vt * 128:vt * 128 + 128], idn[:], start=True, stop=True),
                            inc=mm)
                    p.wait("vector", mm)
                    p.on("vector", lambda E, g4=g4, bank=bank: E.tensor_copy(
                        aot[:, g4 * 512:(g4 + 1) * 512], bank[:, 0:512]), inc=vv)
                aot_done = vv.n
                p.wait("tensor", vv, aot_done)
                for vt in range(ND):
                    for tb in range(4):
                        p.on("tensor", lambda E, vt=vt, tb=tb: E.matmul(
                            pso[tb][:], aot[:, vt * 128:(vt + 1) * 128],
                            wos[:, vt * DM + tb * 512: vt * DM + (tb + 1) * 512],
                            start=(vt == 0), stop=(vt == ND - 1)),
                            inc=(mm if (vt == ND - 1 and tb == 3) else None))
                p.wait("vector", mm)
                if tt >= 1:
                    p.wait("vector", hw)
                for tb in range(4):
                    p.on("vector", lambda E, xr=xr, tb=tb: E.scalar_tensor_tensor(
                        yrow[:, tb * 512:(tb + 1) * 512], pso[tb][:], 1.0,
                        xr[:, tb * 512:(tb + 1) * 512], op0=ALU.mult, op1=ALU.add), inc=vv)
                resid_done[tt] = vv.n
                p.wait("scalar", hw)
                ln_apply(p, None, "", yrow, x3row, lnscr, cols, epscol)
                p.wait("sync", aa)
                p.dma("sync", hw, dr["x3s"][tt * 128:(tt + 1) * 128, :], x3row[:])
            p.wait("sync", hw)
            p.wait("gpsimd", hw)


def build_l1a_program(stop_after=None):
    nc = bass.Bass("TRN2", target_bir_lowering=False)
    dr = {}
    dr["x2s"] = nc.dram_tensor("x2s", [T, DM], F32, kind="ExternalInput")
    dr["wqg"] = nc.dram_tensor("wqg", [4, 16, 128, 768], F32, kind="ExternalInput")
    dr["wkv"] = nc.dram_tensor("wkv", [4, 16, 128, 768], F32, kind="ExternalInput")
    dr["wo"] = nc.dram_tensor("wo", [2048, 2048], F32, kind="ExternalInput")
    dr["lsal"] = nc.dram_tensor("lsal", [128, 64], F32, kind="ExternalInput")
    dr["ident"] = nc.dram_tensor("ident", [128, 128], BF16, kind="ExternalInput")
    dr["aggOs"] = nc.dram_tensor("aggOs", [4, 16, 128, 512], BF16)
    dr["x3s"] = nc.dram_tensor("x3s", [T, DM], F32, kind="ExternalOutput")
    es = ExitStack()
    with es:
        p = Plan(nc, es)
        emit_l1a(p, dr, stop_after=stop_after)
        p.emit()
    return nc


def l1a_host_inputs(inputs):
    Wqg = np.asarray(inputs["L1_attn_Wqg"], np.float32)
    Wkv = np.asarray(inputs["L1_attn_Wkv"], np.float32)
    H, DKq, DVv = 16, 64, 128
    qg, kv = [], []
    for g in range(4):
        hs = range(4 * g, 4 * g + 4)
        qcols = np.concatenate([np.arange(h * 192, h * 192 + 64) for h in hs])
        gcols = np.concatenate([np.arange(h * 192 + 64, (h + 1) * 192) for h in hs])
        cols = np.concatenate([qcols, gcols])
        qg.append(Wqg[:, cols].reshape(16, 128, 768))
        kv.append(Wkv[:, cols].reshape(16, 128, 768))
    return {
        "wqg": np.ascontiguousarray(np.stack(qg)),
        "wkv": np.ascontiguousarray(np.stack(kv)),
        "wo": np.ascontiguousarray(np.asarray(inputs["L1_attn_Wo"], np.float32)),
        "lsal": np.ascontiguousarray(np.tile(np.concatenate([
            np.asarray(inputs["L1_attn_ls_iaw"], np.float32).reshape(1, 16),
            np.asarray(inputs["L1_attn_ls_hws"], np.float32).reshape(1, 16),
            np.zeros((1, 32), np.float32)], axis=1), (128, 1))),
        "ident": np.eye(128, dtype=BF),
    }




def _emit_fp4_decode(p, nc, slab, w, scol_f32, ngroups, dec, vv, aa):
    NB = slab.shape[1]
    lo, hi, t8, s8, bits, tf, cf, sgn, half = (
        dec["lo"], dec["hi"], dec["t8"], dec["s8"], dec["bits"],
        dec["tf"], dec["cf"], dec["sgn"], dec["half"])
    p.on("vector", lambda E: E.tensor_scalar(lo[:, 0:NB], slab[:], 15, None,
                                             op0=ALU.bitwise_and), inc=vv)
    p.on("vector", lambda E: E.tensor_scalar(hi[:, 0:NB], slab[:], 4, None,
                                             op0=ALU.logical_shift_right), inc=vv)
    for pl, src in ((0, lo), (1, hi)):
        p.wait("vector", vv)
        p.on("vector", lambda E, src=src: E.tensor_scalar(
            t8[:, 0:NB], src[:, 0:NB], 7, None, op0=ALU.bitwise_and), inc=vv)
        p.on("vector", lambda E, src=src: E.tensor_scalar(
            s8[:, 0:NB], src[:, 0:NB], 3, None, op0=ALU.logical_shift_right), inc=vv)
        p.wait("vector", vv)
        p.on("vector", lambda E: E.tensor_scalar(
            bits[:, 0:NB], t8[:, 0:NB], 252, 64, op0=ALU.add, op1=ALU.mult), inc=vv)
        p.on("vector", lambda E: E.tensor_copy(tf[:, 0:NB], t8[:, 0:NB]), inc=vv)
        p.on("vector", lambda E: E.tensor_scalar(
            sgn[:, 0:NB], s8[:, 0:NB], -2.0, 1.0, op0=ALU.mult, op1=ALU.add), inc=vv)
        p.wait("scalar", vv)
        p.on("scalar", lambda E: E.activation(
            cf[:, 0:NB], tf[:, 0:NB], AF.Relu, bias=half[:], scale=-0.25), inc=aa)
        p.wait("vector", aa)
        p.on("vector", lambda E, pl=pl: E.scalar_tensor_tensor(
            w[:, pl:2 * NB:2], cf[:, 0:NB], -1.0, bits[:, 0:NB].bitcast(BF16),
            op0=ALU.mult, op1=ALU.add), inc=vv)
        p.wait("vector", vv)
        p.on("vector", lambda E, pl=pl: E.tensor_tensor(
            w[:, pl:2 * NB:2], w[:, pl:2 * NB:2], sgn[:, 0:NB], op=ALU.mult), inc=vv)
    p.wait("vector", vv)
    for r in range(ngroups):
        p.on("vector", lambda E, r=r: E.tensor_scalar(
            w[:, r * 128:(r + 1) * 128], w[:, r * 128:(r + 1) * 128],
            scol_f32[:, r:r + 1], None, op0=ALU.mult), inc=vv)


def emit_l1f(p: Plan, dr, stop_after=None):
    nc, es = p.nc, p.es
    I = 5632
    NI = 44
    NJ = 44

    ld = p.sem("ld")
    hw = p.sem("hw")
    mm = p.sem("mm")
    vv = p.sem("vv")
    aa = p.sem("aa")

    def vs():
        p.wait("vector", vv)

    def asy():
        p.wait("scalar", aa)

    with ExitStack() as st:
        e = st.enter_context
        idn = e(nc.sbuf_tensor("q_idn", [128, 128], BF16))
        epscol = e(nc.sbuf_tensor("q_eps", [128, 1], F32))
        halfc = e(nc.sbuf_tensor("q_half", [128, 1], F32))
        rmsP = e(nc.sbuf_tensor("q_rmsP", [128, 48], F32))

        p.dma("sync", hw, idn[:], dr["ident"][:, :])
        p.on("vector", lambda E: E.memset(epscol[:], 1e-5), inc=vv)
        p.on("vector", lambda E: E.memset(halfc[:], 0.5), inc=vv)
        p.on("vector", lambda E: E.memset(rmsP[:], 0.0), inc=vv)

        s01 = st.enter_context(ExitStack())
        x3T = s01.enter_context(nc.sbuf_tensor("q_x3T", [128, NT * T], BF16))
        with ExitStack() as s0:
            e0 = s0.enter_context
            xrow = [e0(nc.sbuf_tensor(f"q_xr{i}", [128, DM], BF16)) for i in range(2)]
            xrowf = [e0(nc.sbuf_tensor(f"q_xrf{i}", [128, DM], F32)) for i in range(2)]
            ps_t = [e0(nc.psum_tensor(f"q_pst{i}", [128, 512], F32)) for i in range(2)]
            p.wait("tensor", hw)
            x3Tv = x3T[:].rearrange("p (dt tc) -> p dt tc", dt=ND)
            evd = {}
            cvt0 = {}
            for tt in range(NT):
                b = xrow[tt % 2]
                bf_ = xrowf[tt % 2]
                if tt >= 2:
                    p.wait("sync", aa, cvt0[tt - 2])
                p.dma("sync", hw, bf_[:], dr["x3s"][tt * 128:(tt + 1) * 128, :])
                p.wait("scalar", hw)
                if tt >= 2:
                    p.wait("scalar", mm)
                p.on("scalar", lambda E, b=b, bf_=bf_: E.activation(
                    b[:], bf_[:], AF.Identity), inc=aa)
                cvt0[tt] = aa.n
                p.wait("tensor", aa, cvt0[tt])
                for g4 in range(4):
                    bank = ps_t[g4 % 2]
                    if tt * 4 + g4 >= 2:
                        p.wait("tensor", vv, vv.n - 1)
                    for j in range(4):
                        dt = 4 * g4 + j
                        p.on("tensor",
                             lambda E, b=b, dt=dt, j=j, bank=bank: E.matmul(
                                 bank[:, j * 128:j * 128 + 128],
                                 b[:, dt * 128:dt * 128 + 128], idn[:], start=True, stop=True),
                             inc=mm)
                    p.wait("vector", mm)
                    p.on("vector",
                         lambda E, g4=g4, tt=tt, bank=bank: E.tensor_copy(
                             x3Tv[:, 4 * g4:4 * g4 + 4, tt * 128:tt * 128 + 128],
                             bank[:, 0:512].rearrange("p (j tc) -> p j tc", j=4)),
                         inc=vv)
                evd[tt] = vv.n
        x3T_done = vv.n
        if stop_after == "x3T":
            p.wait("gpsimd", vv, x3T_done)
            for dt in range(ND):
                p.dma("gpsimd", ld, dr["x4s"][dt * 128:(dt + 1) * 128, :],
                      x3T[:, dt * T:(dt + 1) * T])
            p.wait("gpsimd", ld)
            return

        with ExitStack() as s1:
            e1 = s1.enter_context
            slab = [e1(nc.sbuf_tensor(f"q_sl{i}", [128, 1024], U8)) for i in range(2)]
            sct = [e1(nc.sbuf_tensor(f"q_sc{i}", [128, 16], mybir.dt.float16)) for i in range(2)]
            scf = [e1(nc.sbuf_tensor(f"q_scf{i}", [128, 16], F32)) for i in range(2)]
            dec = {k: e1(nc.sbuf_tensor(f"q_d{k}", [128, 1024],
                                        U8 if k in ("lo", "hi", "t8", "s8") else
                                        (U16 if k == "bits" else
                                         (F32 if k == "tf" else BF16))))
                   for k in ("lo", "hi", "t8", "s8", "bits", "tf", "cf", "sgn")}
            dec["half"] = halfc
            wg16 = e1(nc.sbuf_tensor("q_wg", [128, 2048], BF16))
            wu16 = e1(nc.sbuf_tensor("q_wu", [128, 2048], BF16))
            wTg = e1(nc.sbuf_tensor("q_wTg", [128, 2048], BF16))
            wTu = e1(nc.sbuf_tensor("q_wTu", [128, 2048], BF16))
            s1b = e1(nc.sbuf_tensor("q_s1b", [128, 1024], BF16))
            hbuf = e1(nc.sbuf_tensor("q_hb", [128, T], BF16))
            ps_g = [e1(nc.psum_tensor(f"q_pg{i}", [128, 512], F32)) for i in range(2)]
            ps_u = [e1(nc.psum_tensor(f"q_pu{i}", [128, 512], F32)) for i in range(2)]
            ps_tr = [e1(nc.psum_tensor(f"q_ptr{i}", [128, 512], F32)) for i in range(2)]

            p.wait("gpsimd", mm)
            p.wait("gpsimd", hw)
            dq_done = {}
            mm_done = {}
            hst = {}
            NJrun = 1 if stop_after in ("gu0", "dq0") else NJ
            for j in range(NJrun):
                for wi, (wdq, wT) in enumerate(((wg16, wTg), (wu16, wTu))):
                    ot = j + wi * NJ
                    sb = slab[wi]
                    sc = sct[wi]
                    sf = scf[wi]
                    if j >= 1:
                        p.wait("gpsimd", vv, dq_done[(j - 1, wi)])
                    p.dma("gpsimd", ld, sb[:], dr["gup"][ot, :, :])
                    p.dma("gpsimd", ld, sc[:], dr["gusc"][ot, :, :])
                    p.wait("vector", ld)
                    if j >= 1:
                        p.wait("vector", mm, mm_done[(j - 1, wi, "tr")])
                    p.on("vector", lambda E, sf=sf, sc=sc: E.tensor_copy(sf[:], sc[:]), inc=vv)
                    vs()
                    _emit_fp4_decode(p, nc, sb, wdq, sf, 16, dec, vv, aa)
                    dq_done[(j, wi)] = vv.n
                    p.wait("tensor", vv, dq_done[(j, wi)])
                    if j >= 1:
                        p.wait("tensor", mm, mm_done[(j - 1, wi, "mm")])
                    for g4 in range(4):
                        bank = ps_tr[g4 % 2]
                        if (j * 2 + wi) * 4 + g4 >= 2:
                            p.wait("tensor", vv, vv.n - 1)
                        for jj in range(4):
                            dt = 4 * g4 + jj
                            p.on("tensor", lambda E, wdq=wdq, dt=dt, jj=jj, bank=bank: E.matmul(
                                bank[:, jj * 128:(jj + 1) * 128],
                                wdq[:, dt * 128:(dt + 1) * 128], idn[:], start=True, stop=True),
                                inc=mm)
                        p.wait("vector", mm)
                        p.on("vector", lambda E, wT=wT, g4=g4, bank=bank: E.tensor_copy(
                            wT[:, g4 * 512:(g4 + 1) * 512], bank[:, 0:512]), inc=vv)
                    mm_done[(j, wi, "trv")] = vv.n
                    mm_done[(j, wi, "tr")] = mm.n
                    if stop_after == "dec" and j == 0 and wi == 0:
                        p.wait("gpsimd", vv)
                        p.dma("gpsimd", ld, dr["x4s"][0:128, :], wdq[:])
                        p.wait("gpsimd", ld)
                        return
                    if stop_after == "dqw" and j == 0 and wi == 0:
                        p.wait("gpsimd", vv)
                        p.dma("gpsimd", ld, dr["x4s"][0:128, :], wT[:])
                        p.wait("gpsimd", ld)
                        return
                p.wait("tensor", vv, mm_done[(j, 1, "trv")])
                if j >= 1:
                    p.wait("tensor", aa)
                    p.wait("tensor", vv)
                for th in range(2):
                    if th == 1:
                        p.wait("tensor", aa)
                        p.wait("tensor", vv)
                    for tb in range(2):
                        for dt in range(ND):
                            p.on("tensor", lambda E, dt=dt, th=th, tb=tb: E.matmul(
                                ps_g[tb][:], wTg[:, dt * 128:(dt + 1) * 128],
                                x3T[:, dt * T + th * 1024 + tb * 512:
                                    dt * T + th * 1024 + (tb + 1) * 512],
                                start=(dt == 0), stop=(dt == ND - 1)))
                            p.on("tensor", lambda E, dt=dt, th=th, tb=tb: E.matmul(
                                ps_u[tb][:], wTu[:, dt * 128:(dt + 1) * 128],
                                x3T[:, dt * T + th * 1024 + tb * 512:
                                    dt * T + th * 1024 + (tb + 1) * 512],
                                start=(dt == 0), stop=(dt == ND - 1)),
                                inc=(mm if dt == ND - 1 else None))
                    gum = mm.n
                    p.wait("scalar", mm, gum)
                    if th == 1:
                        p.wait("scalar", vv)
                    p.on("scalar", lambda E, th=th: E.activation(
                        s1b[:], ps_g[0][:].rearrange("p f -> p f") if False else ps_g[0][:],
                        AF.Silu)) if False else None
                    for tb in range(2):
                        p.on("scalar", lambda E, tb=tb: E.activation(
                            s1b[:, tb * 512:(tb + 1) * 512], ps_g[tb][:], AF.Silu), inc=aa)
                    p.wait("vector", aa)
                    if th == 0 and j >= 1:
                        p.wait("vector", hw, hst[j - 1])
                    for tb in range(2):
                        p.on("vector", lambda E, th=th, tb=tb: E.tensor_tensor(
                            hbuf[:, th * 1024 + tb * 512: th * 1024 + (tb + 1) * 512],
                            s1b[:, tb * 512:(tb + 1) * 512], ps_u[tb][:], op=ALU.mult), inc=vv)
                mm_done[(j, 0, "mm")] = mm.n
                mm_done[(j, 1, "mm")] = mm.n
                p.wait("sync", vv)
                p.dma("sync", hw, dr["hs2"][j, :, :], hbuf[:])
                hst[j] = hw.n
                if stop_after == "dq0" and j == 0:
                    p.wait("gpsimd", vv)
                    p.dma("gpsimd", ld, dr["x4s"][0:128, :], wTg[:])
                    p.dma("gpsimd", ld, dr["x4s"][128:256, :], wTu[:])
                    p.wait("gpsimd", ld)
                    return
                if stop_after == "gu0" and j == 0:
                    p.wait("gpsimd", hw)
                    p.dma("gpsimd", ld, dr["x4s"][0:128, :], hbuf[:])
                    p.wait("gpsimd", ld)
                    return
            h_all = hw.n
        s01.close()

        for hhalf in range(2):
            with ExitStack() as s2:
                e2 = s2.enter_context
                wdT = e2(nc.sbuf_tensor(f"q_wdT{hhalf}", [128, NI * 1024], BF16))
                s2d = s2.enter_context(ExitStack())
                e2d = s2d.enter_context
                dslab = e2d(nc.sbuf_tensor(f"q_ds{hhalf}", [128, 2816], U8))
                dsct = e2d(nc.sbuf_tensor(f"q_dsc{hhalf}", [128, 44], mybir.dt.float16))
                dscf = e2d(nc.sbuf_tensor(f"q_dscf{hhalf}", [128, 44], F32))
                dec2 = {k: e2d(nc.sbuf_tensor(f"q_e{hhalf}{k}", [128, 2816],
                                             U8 if k in ("lo", "hi", "t8", "s8") else
                                             (U16 if k == "bits" else
                                              (F32 if k == "tf" else BF16))))
                       for k in ("lo", "hi", "t8", "s8", "bits", "tf", "cf", "sgn")}
                dec2["half"] = halfc
                wdq2 = e2d(nc.sbuf_tensor(f"q_wd{hhalf}", [128, I], BF16))
                ps_tr2 = [e2(nc.psum_tensor(f"q_pt2{hhalf}_{i}", [128, 512], F32))
                          for i in range(2)]

                p.wait("gpsimd", mm)
                p.wait("gpsimd", hw)
                p.wait("gpsimd", vv)
                for ht in range(8):
                    hta = hhalf * 8 + ht
                    if ht >= 1:
                        p.wait("gpsimd", vv)
                    p.dma("gpsimd", ld, dslab[:], dr["dnp"][hta, :, :])
                    p.dma("gpsimd", ld, dsct[:], dr["dnsc"][hta, :, :])
                    p.wait("vector", ld)
                    if ht >= 1:
                        p.wait("vector", mm)
                    p.on("vector", lambda E: E.tensor_copy(dscf[:], dsct[:]), inc=vv)
                    vs()
                    _emit_fp4_decode(p, nc, dslab, wdq2, dscf, NI, dec2, vv, aa)
                    dqd = vv.n
                    p.wait("tensor", vv, dqd)
                    for g4 in range(NI // 4):
                        bank = ps_tr2[g4 % 2]
                        if ht * 11 + g4 >= 2:
                            p.wait("tensor", vv, vv.n - 1)
                        for jj in range(4):
                            it = 4 * g4 + jj
                            p.on("tensor", lambda E, it=it, jj=jj, bank=bank: E.matmul(
                                bank[:, jj * 128:(jj + 1) * 128],
                                wdq2[:, it * 128:(it + 1) * 128], idn[:], start=True, stop=True),
                                inc=mm)
                        p.wait("vector", mm)
                        wdTv = wdT[:].rearrange("p (it hc) -> p it hc", it=NI)
                        p.on("vector", lambda E, g4=g4, ht=ht, wdTv=wdTv, bank=bank: E.tensor_copy(
                            wdTv[:, 4 * g4:4 * g4 + 4, ht * 128:(ht + 1) * 128],
                            bank[:, 0:512].rearrange("p (jj f) -> p jj f", jj=4)),
                            inc=vv)
                wdT_done = vv.n
                s2d.close()

                with ExitStack() as s2b:
                    e2b = s2b.enter_context
                    h_sb = e2b(nc.sbuf_tensor(f"q_hsb{hhalf}", [128, NI * 1024], BF16))
                    yb = [e2b(nc.sbuf_tensor(f"q_yb{hhalf}{i}", [128, 1024], BF16))
                          for i in range(2)]
                    sqs = e2b(nc.sbuf_tensor(f"q_sqs{hhalf}", [128, 1024], BF16))
                    psY = [e2b(nc.psum_tensor(f"q_py{hhalf}_{i}", [128, 512], F32))
                           for i in range(4)]
                    for th in range(2):
                        p.wait("sync", hw, h_all)
                        p.wait("sync", vv, wdT_done)
                        if th == 1:
                            p.wait("sync", mm)
                        p.dma("sync", hw,
                              h_sb[:].rearrange("p (it tc) -> p it tc", it=NI),
                              dr["hs2"][:, :, th * 1024:(th + 1) * 1024]
                              .rearrange("it p tc -> p it tc"))
                        hld = hw.n
                        p.wait("tensor", hw, hld)
                        p.wait("tensor", vv, wdT_done)
                        for tt8 in range(8):
                            tt = th * 8 + tt8
                            pb = (tt8 % 2) * 2
                            if tt8 >= 2:
                                p.wait("tensor", vv)
                            for it in range(NI):
                                for tb in range(2):
                                    p.on("tensor", lambda E, it=it, tt8=tt8, tb=tb, pb=pb: E.matmul(
                                        psY[pb + tb][:],
                                        h_sb[:, it * 1024 + tt8 * 128: it * 1024 + (tt8 + 1) * 128],
                                        wdT[:, it * 1024 + tb * 512: it * 1024 + (tb + 1) * 512],
                                        start=(it == 0), stop=(it == NI - 1)),
                                        inc=(mm if (it == NI - 1 and tb == 1) else None))
                            ymm = mm.n
                            b = yb[tt8 % 2]
                            p.wait("vector", mm, ymm)
                            if tt8 >= 2:
                                p.wait("vector", hw)
                            for tb in range(2):
                                p.on("vector", lambda E, b=b, tb=tb, pb=pb: E.tensor_copy(
                                    b[:, tb * 512:(tb + 1) * 512], psY[pb + tb][:]), inc=vv)
                            vs()
                            p.on("vector", lambda E, b=b, tt=tt, hhalf=hhalf: E.scalar_tensor_tensor(
                                sqs[:], b[:], 1.0, b[:], op0=ALU.mult, op1=ALU.mult,
                                accum_out=rmsP[:, hhalf * 16 + tt: hhalf * 16 + tt + 1]), inc=vv)
                            p.wait("sync", vv)
                            p.dma("sync", hw,
                                  dr["ys2"][tt, :, hhalf * 1024:(hhalf + 1) * 1024], b[:])
        y_all2 = hw.n
        if stop_after == "down":
            p.wait("sync", hw)
            p.wait("gpsimd", hw)
            return

        with ExitStack() as s3:
            e3 = s3.enter_context
            ybig = [e3(nc.sbuf_tensor(f"q_ybig{i}", [128, DM], BF16)) for i in range(2)]
            x3r = [e3(nc.sbuf_tensor(f"q_x3r{i}", [128, DM], F32)) for i in range(2)]
            yrow = e3(nc.sbuf_tensor("q_yrow", [128, DM], BF16))
            x4row = e3(nc.sbuf_tensor("q_x4row", [128, DM], F32))
            lnscr = e3(nc.sbuf_tensor("q_lscr", [128, DM], BF16))
            cols = e3(nc.sbuf_tensor("q_cols", [128, 8], F32))
            rcol = e3(nc.sbuf_tensor("q_rcol", [128, 2], F32))

            p.wait("sync", hw, y_all2)
            for engx in ("vector", "scalar"):
                p.wait(engx, hw, y_all2)
            for tt in range(NT):
                b = ybig[tt % 2]
                xr = x3r[tt % 2]
                if tt >= 2:
                    p.wait("sync", vv)
                p.dma("sync", hw, b[:], dr["ys2"][tt, :, :])
                p.dma("sync", hw, xr[:], dr["x3s"][tt * 128:(tt + 1) * 128, :])
                p.wait("vector", hw)
                p.on("vector", lambda E, tt=tt: E.scalar_tensor_tensor(
                    rcol[:, 0:1], rmsP[:, tt:tt + 1], 1.0 / I, rmsP[:, 16 + tt:16 + tt + 1],
                    op0=ALU.bypass, op1=ALU.add) if False else E.tensor_tensor(
                    rcol[:, 0:1], rmsP[:, tt:tt + 1], rmsP[:, 16 + tt:16 + tt + 1],
                    op=ALU.add), inc=vv)
                vs()
                p.on("vector", lambda E: E.tensor_scalar(
                    rcol[:, 0:1], rcol[:, 0:1], 1.0 / DM, 1e-6,
                    op0=ALU.mult, op1=ALU.add), inc=vv)
                p.wait("scalar", vv)
                p.on("scalar", lambda E: E.activation(rcol[:, 1:2], rcol[:, 0:1], AF.Ln), inc=aa)
                asy()
                p.on("scalar", lambda E: E.activation(rcol[:, 1:2], rcol[:, 1:2],
                                                      AF.Exp, scale=-0.5), inc=aa)
                p.wait("vector", aa)
                p.on("vector", lambda E, b=b, xr=xr: E.scalar_tensor_tensor(
                    yrow[:], b[:], rcol[:, 1:2], xr[:], op0=ALU.mult, op1=ALU.add), inc=vv)
                p.wait("scalar", hw)
                ln_apply(p, None, "", yrow, x4row, lnscr, cols, epscol)
                p.wait("sync", aa)
                p.dma("sync", hw, dr["x4s"][tt * 128:(tt + 1) * 128, :], x4row[:])
            p.wait("sync", hw)
            p.wait("gpsimd", hw)


def build_l1f_program(stop_after=None, debug=False):
    nc = bass.Bass("TRN2", target_bir_lowering=False)
    k = dict(kind="ExternalOutput") if debug else {}
    dr = {}
    dr["x3s"] = nc.dram_tensor("x3s", [T, DM], F32, kind="ExternalInput")
    dr["gup"] = nc.dram_tensor("gup", [88, 128, 1024], U8, kind="ExternalInput")
    dr["gusc"] = nc.dram_tensor("gusc", [88, 128, 16], mybir.dt.float16, kind="ExternalInput")
    dr["dnp"] = nc.dram_tensor("dnp", [16, 128, 2816], U8, kind="ExternalInput")
    dr["dnsc"] = nc.dram_tensor("dnsc", [16, 128, 44], mybir.dt.float16, kind="ExternalInput")
    dr["ident"] = nc.dram_tensor("ident", [128, 128], BF16, kind="ExternalInput")
    dr["hs2"] = nc.dram_tensor("hs2", [44, 128, T], BF16, **k)
    dr["ys2"] = nc.dram_tensor("ys2", [16, 128, DM], BF16, **k)
    dr["x4s"] = nc.dram_tensor("x4s", [T, DM], F32, kind="ExternalOutput")
    es = ExitStack()
    with es:
        p = Plan(nc, es)
        emit_l1f(p, dr, stop_after=stop_after)
        p.emit()
    return nc


def l1f_host_inputs(inputs):
    gup = np.asarray(inputs["L1_ffn_gate_up_packed"])
    gusc = np.asarray(inputs["L1_ffn_gate_up_scales"])
    dnp = np.asarray(inputs["L1_ffn_down_packed"])
    dnsc = np.asarray(inputs["L1_ffn_down_scales"])
    return {
        "gup": np.ascontiguousarray(gup.view(np.uint8).reshape(88, 128, 1024)),
        "gusc": np.ascontiguousarray(gusc.T.reshape(88, 128, 16)),
        "dnp": np.ascontiguousarray(dnp.view(np.uint8).reshape(16, 128, 2816)),
        "dnsc": np.ascontiguousarray(dnsc.T.reshape(16, 128, 44)),
        "ident": np.eye(128, dtype=BF),
    }




def emit_l0a_A(p: Plan, dr, stop_after=None):
    nc, es = p.nc, p.es

    ld = p.sem("ld")
    hw = p.sem("hw")
    mm = p.sem("mm")
    vv = p.sem("vv")
    aa = p.sem("aa")

    def vs():
        p.wait("vector", vv)

    def asy():
        p.wait("scalar", aa)

    with ExitStack() as st:
        e = st.enter_context
        idn = e(nc.sbuf_tensor("A_idn", [128, 128], BF16))
        idf = e(nc.sbuf_tensor("A_idf", [128, 128], F32))
        onec = e(nc.sbuf_tensor("A_onec", [128, 1], F32))
        xT = e(nc.sbuf_tensor("A_xT", [128, NT * T], BF16))

        p.dma("sync", hw, idn[:], dr["ident"][:, :])
        p.dma("sync", hw, idf[:], dr["idf32"][:, :])
        p.on("vector", lambda E: E.memset(onec[:], 1.0), inc=vv)

        if stop_after == "x0":
            with ExitStack() as sx:
                xb = sx.enter_context(nc.sbuf_tensor("A_x0d", [128, DM], BF16))
                for tt in range(NT):
                    p.wait("gpsimd", ld)
                    p.dma("gpsimd", ld, xb[:], dr["x0s"][tt * 128:(tt + 1) * 128, :])
                    p.wait("gpsimd", ld)
                    p.dma("gpsimd", ld, dr["qkvc"][tt, :, :], xb[:])
                p.wait("gpsimd", ld)
            return

        with ExitStack() as s0:
            e0 = s0.enter_context
            xrowf = [e0(nc.sbuf_tensor(f"A_xrf{i}", [128, DM], F32)) for i in range(2)]
            xrow = [e0(nc.sbuf_tensor(f"A_xr{i}", [128, DM], BF16)) for i in range(2)]
            ps_t = [e0(nc.psum_tensor(f"A_pst{i}", [128, 512], F32)) for i in range(2)]
            p.wait("tensor", hw)
            xTv = xT[:].rearrange("p (dt tc) -> p dt tc", dt=ND)
            evd = {}
            cvt = {}
            for tt in range(NT):
                bf_ = xrowf[tt % 2]
                b = xrow[tt % 2]
                if tt >= 2:
                    p.wait("sync", aa, cvt[tt - 2])
                p.dma("sync", hw, bf_[:], dr["x0s"][tt * 128:(tt + 1) * 128, :])
                p.wait("scalar", hw)
                if tt >= 2:
                    p.wait("scalar", mm)
                p.on("scalar", lambda E, b=b, bf_=bf_: E.activation(
                    b[:], bf_[:], AF.Identity), inc=aa)
                cvt[tt] = aa.n
                p.wait("tensor", aa, cvt[tt])
                for g4 in range(4):
                    bank = ps_t[g4 % 2]
                    if tt * 4 + g4 >= 2:
                        p.wait("tensor", vv, vv.n - 1)
                    for j in range(4):
                        dt = 4 * g4 + j
                        p.on("tensor",
                             lambda E, b=b, dt=dt, j=j, bank=bank: E.matmul(
                                 bank[:, j * 128:j * 128 + 128],
                                 b[:, dt * 128:dt * 128 + 128], idn[:], start=True, stop=True),
                             inc=mm)
                    p.wait("vector", mm)
                    p.on("vector",
                         lambda E, g4=g4, tt=tt, bank=bank: E.tensor_copy(
                             xTv[:, 4 * g4:4 * g4 + 4, tt * 128:tt * 128 + 128],
                             bank[:, 0:512].rearrange("p (j tc) -> p j tc", j=4)),
                         inc=vv)
                evd[tt] = vv.n
        xT_done = vv.n
        if stop_after == "xT":
            p.wait("gpsimd", vv, xT_done)
            for dt in range(ND):
                p.dma("gpsimd", ld, dr["qkvc"][dt, :, :], xT[:, dt * T:(dt + 1) * T])
            p.wait("gpsimd", ld)
            return

        with ExitStack() as s2:
            e2 = s2.enter_context
            wba = e2(nc.sbuf_tensor("A_wba", [128, ND * 64], BF16))
            wbaf = e2(nc.sbuf_tensor("A_wbaf", [128, ND * 64 + 64], F32))
            bat = e2(nc.sbuf_tensor("A_bat", [128, NT * 64], F32))
            nabc = e2(nc.sbuf_tensor("A_nabc", [128, 32], F32))
            grow = e2(nc.sbuf_tensor("A_grow", [128, 6 * T], F32))
            gcs = e2(nc.sbuf_tensor("A_gcs", [128, 64], F32))
            scrA = e2(nc.sbuf_tensor("A_scrA", [128, 32], F32))
            ps_b = [e2(nc.psum_tensor(f"A_psb{i}", [128, 512], F32)) for i in range(2)]
            ps_r = e2(nc.psum_tensor("A_psr", [128, 512], F32))

            p.dma("sync", hw, wbaf[:], dr["wba"][:, :])
            dtal = wbaf[:, ND * 64: ND * 64 + 64]
            dtbc = dtal[:, 0:32]
            p.wait("scalar", ld)
            p.wait("scalar", hw)
            p.on("scalar", lambda E: E.activation(wba[:], wbaf[:, 0:ND * 64], AF.Identity), inc=aa)
            p.on("scalar", lambda E: E.activation(nabc[:], dtal[:, 32:64], AF.Exp), inc=aa)
            p.wait("vector", aa)
            p.on("vector", lambda E: E.tensor_scalar(nabc[:], nabc[:], -1.0, None,
                                                     op0=ALU.mult), inc=vv)
            p.wait("tensor", ld)
            p.wait("tensor", hw)
            p.wait("tensor", vv, xT_done)
            bev = {}
            if stop_after in ("ba0", "ba0b"):
                p.wait("sync", vv)
                p.wait("sync", ld)
                p.wait("sync", aa)
                for dt in range(ND):
                    p.dma("sync", hw, dr["qkvc"][dt, :, :], xT[:, dt * T:(dt + 1) * T])
                p.wait("sync", hw)
                return
            for tt in range(NT):
                pb = ps_b[(tt + 1) % 2]
                if tt >= 2:
                    p.wait("tensor", vv, bev[tt - 2])
                for dt in range(ND):
                    p.on("tensor", lambda E, tt=tt, dt=dt, pb=pb: E.matmul(
                        pb[:, 0:64], xT[:, dt * T + tt * 128: dt * T + tt * 128 + 128],
                        wba[:, dt * 64:(dt + 1) * 64],
                        start=(dt == 0), stop=(dt == ND - 1)),
                        inc=(mm if dt == ND - 1 else None))
                p.wait("vector", mm)
                p.on("vector", lambda E, tt=tt, pb=pb: E.tensor_copy(
                    bat[:, tt * 64:(tt + 1) * 64], pb[:, 0:64]), inc=vv)
                bev[tt] = vv.n
            vs()
            if stop_after == "ba1":
                p.wait("sync", vv)
                for tt in range(NT):
                    p.dma("sync", hw, dr["betaT"][tt * 128:(tt + 1) * 128, :],
                          bat[:, tt * 64:tt * 64 + 32])
                    p.dma("sync", hw, dr["lgT"][tt * 128:(tt + 1) * 128, :],
                          bat[:, tt * 64 + 32:tt * 64 + 64])
                for dt in range(ND):
                    p.dma("sync", hw, dr["qkvc"][dt, :, :], xT[:, dt * T:(dt + 1) * T])
                p.wait("sync", hw)
                return
            p.wait("scalar", vv)
            for tt in range(NT):
                p.on("scalar", lambda E, tt=tt: E.activation(
                    bat[:, tt * 64:tt * 64 + 32], bat[:, tt * 64:tt * 64 + 32],
                    AF.Sigmoid), inc=aa)
            p.wait("sync", aa)
            for tt in range(NT):
                p.dma("sync", hw, dr["betaT"][tt * 128:(tt + 1) * 128, :],
                      bat[:, tt * 64:tt * 64 + 32])
            for tt in range(NT):
                a0 = bat[:, tt * 64 + 32:tt * 64 + 64]
                p.on("vector", lambda E, a0=a0: E.tensor_tensor(a0, a0, dtbc[:], op=ALU.add), inc=vv)
            vs()
            p.wait("scalar", vv)
            for tt in range(NT):
                a0 = bat[:, tt * 64 + 32:tt * 64 + 64]
                p.on("scalar", lambda E, a0=a0: E.activation(scrA[:], a0, AF.Abs), inc=aa)
                asy()
                p.on("scalar", lambda E: E.activation(scrA[:], scrA[:], AF.Exp, scale=-1.0), inc=aa)
                asy()
                p.on("scalar", lambda E: E.activation(scrA[:], scrA[:], AF.Ln, bias=onec[:]), inc=aa)
                p.wait("vector", aa)
                p.on("vector", lambda E, a0=a0: E.tensor_scalar(a0, a0, 0.0, None,
                                                                op0=ALU.max), inc=vv)
                vs()
                p.on("vector", lambda E, a0=a0: E.tensor_tensor(a0, a0, scrA[:], op=ALU.add), inc=vv)
                vs()
                p.on("vector", lambda E, a0=a0: E.tensor_tensor(a0, a0, nabc[:], op=ALU.mult), inc=vv)
                vs()
                p.wait("scalar", vv)
            g_done = vv.n
            if stop_after == "ba2":
                p.wait("sync", vv, g_done)
                for tt in range(NT):
                    p.dma("sync", hw, dr["lgT"][tt * 128:(tt + 1) * 128, :],
                          bat[:, tt * 64 + 32:tt * 64 + 64])
                p.wait("sync", hw)
                return
            p.wait("tensor", vv, g_done)
            for tt in range(NT):
                if tt >= 1:
                    p.wait("tensor", vv)
                p.on("tensor", lambda E, tt=tt: E.matmul(
                    ps_r[0:32, 0:128], bat[:, tt * 64 + 32:tt * 64 + 64], idf[:], start=True, stop=True),
                    inc=mm)
                p.wait("vector", mm)
                p.on("vector", lambda E, tt=tt: E.tensor_copy(
                    grow[0:32, tt * 128:(tt + 1) * 128], ps_r[0:32, 0:128]), inc=vv)
            p.on("vector", lambda E: E.memset(grow[0:32, 5 * T:6 * T], 0.0), inc=vv)
            vs()
            if stop_after == "ba3":
                p.wait("sync", vv)
                for tt in range(NT):
                    p.dma("sync", hw, dr["lgT"][tt * 128:(tt + 1) * 128, 0:32],
                          grow[0:32, tt * 128:(tt + 1) * 128].rearrange("h t -> h t") if False
                          else grow[0:32, tt * 128:tt * 128 + 32])
                p.wait("sync", hw)
                return
            for c in range(NT):
                p.on("vector", lambda E, c=c: E.tensor_tensor_scan(
                    grow[0:32, T + c * 128: T + (c + 1) * 128],
                    grow[0:32, c * 128:(c + 1) * 128],
                    grow[0:32, 5 * T + c * 128: 5 * T + (c + 1) * 128],
                    0.0, op0=ALU.add, op1=ALU.add), inc=vv)
            vs()
            for c in range(NT):
                lc = grow[0:32, T + c * 128 + 127: T + c * 128 + 128]
                p.on("vector", lambda E, c=c, lc=lc: E.tensor_scalar(
                    grow[0:32, 2 * T + c * 128: 2 * T + (c + 1) * 128],
                    grow[0:32, T + c * 128: T + (c + 1) * 128],
                    -1.0, lc, op0=ALU.mult, op1=ALU.add), inc=vv)
                p.on("vector", lambda E, c=c, lc=lc: E.tensor_copy(
                    gcs[0:32, c:c + 1], lc), inc=vv)
            vs()
            p.wait("scalar", vv)
            p.on("scalar", lambda E: E.activation(
                grow[0:32, 3 * T:4 * T], grow[0:32, T:2 * T], AF.Exp), inc=aa)
            p.on("scalar", lambda E: E.activation(
                grow[0:32, 4 * T:5 * T], grow[0:32, 2 * T:3 * T], AF.Exp), inc=aa)
            p.on("scalar", lambda E: E.activation(
                gcs[0:32, 0:16], gcs[0:32, 0:16], AF.Exp), inc=aa)
            p.wait("tensor", aa)
            p.on("tensor", lambda E: E.matmul(ps_r[0:16, 128:160], gcs[0:32, 0:16],
                                                 idf[0:32, 0:32], start=True, stop=True), inc=mm)
            p.wait("vector", mm)
            p.on("vector", lambda E: E.tensor_copy(gcs[0:16, 32:64], ps_r[0:16, 128:160]), inc=vv)
            vs()
            p.wait("sync", vv)
            p.dma("sync", hw, dr["gamC"][:, :], gcs[0:16, 32:64])
            p.wait("tensor", vv)
            for (src0, dst) in ((T, "lgT"), (3 * T, "elT"), (4 * T, "ervT")):
                for tt in range(NT):
                    p.on("tensor", lambda E, src0=src0, tt=tt: E.matmul(
                        ps_r[:, 256:288], grow[0:32, src0 + tt * 128: src0 + (tt + 1) * 128],
                        idf[0:32, 0:32], start=True, stop=True), inc=mm)
                    p.wait("vector", mm)
                    p.on("vector", lambda E, tt=tt: E.tensor_copy(
                        scrA[:], ps_r[:, 256:288]), inc=vv)
                    p.wait("sync", vv)
                    p.dma("sync", hw, dr[dst][tt * 128:(tt + 1) * 128, :], scrA[:])
                    p.wait("tensor", hw)

        if stop_after == "ba":
            p.wait("sync", hw)
            p.wait("gpsimd", hw)
            return

        with ExitStack() as s3:
            e3 = s3.enter_context
            wsl = [e3(nc.sbuf_tensor(f"A_w{i}", [128, DM], BF16)) for i in range(2)]
            wslf = [e3(nc.sbuf_tensor(f"A_wf{i}", [128, DM], F32)) for i in range(2)]
            convw = e3(nc.sbuf_tensor("A_cw", [128, 256], F32))
            pre = [e3(nc.sbuf_tensor(f"A_pre{i}", [128, T], BF16)) for i in range(2)]
            acc = [e3(nc.sbuf_tensor(f"A_acc{i}", [128, T], BF16)) for i in range(2)]
            qbuf = [e3(nc.sbuf_tensor(f"A_qb{i}", [128, T], BF16)) for i in range(2)]
            psq = [e3(nc.psum_tensor(f"A_psq{i}", [128, 512], F32)) for i in range(8)]

            p.dma("sync", hw, convw[:], dr["convw"][:, :])
            cw_hw = hw.n
            p.wait("vector", hw, cw_hw)
            p.wait("tensor", vv)
            p.wait("tensor", aa)
            if stop_after == "wq":
                p.dma("gpsimd", ld, wsl[0][:], dr["wqkv"][0, :, :])
                p.dma("gpsimd", ld, wsl[1][:], dr["wqkv"][1, :, :])
                p.wait("sync", ld)
                p.dma("sync", hw, dr["qkvc"][0, :, :], wsl[0][:])
                p.dma("sync", hw, dr["qkvc"][1, :, :], wsl[1][:])
                p.wait("sync", hw)
                return
            if stop_after == "wq2":
                with ExitStack() as swq:
                    wf = swq.enter_context(nc.sbuf_tensor("A_wqf", [128, DM], F32))
                    for pt in (0, 1):
                        p.dma("sync", hw, wf[:], dr["wqkv"][pt, :, :])
                        p.wait("scalar", hw)
                        p.on("scalar", lambda E, pt=pt: E.activation(
                            wsl[pt][:], wf[:], AF.Identity), inc=aa)
                        p.wait("sync", aa)
                    p.dma("sync", hw, dr["qkvc"][0, :, :], wsl[0][:])
                    p.dma("sync", hw, dr["qkvc"][1, :, :], wsl[1][:])
                    p.wait("sync", hw)
                return
            mmd = {}
            cvd = {}
            qst = {}
            wcast = {}
            for pt in range(64):
                w = wsl[pt % 2]
                wf = wslf[pt % 2]
                if pt >= 2:
                    p.wait("sync", aa, wcast[pt - 2])
                p.dma("sync", hw, wf[:], dr["wqkv"][pt, :, :])
                p.wait("scalar", hw)
                if pt >= 2:
                    p.wait("scalar", mm, mmd[pt - 2])
                p.on("scalar", lambda E, w=w, wf=wf: E.activation(w[:], wf[:], AF.Identity), inc=aa)
                wcast[pt] = aa.n
                p.wait("tensor", aa, wcast[pt])
                bank0 = (pt % 2) * 4
                if pt >= 2:
                    p.wait("tensor", vv, cvd[pt - 2])
                for dt in range(ND):
                    for tb in range(4):
                        p.on("tensor", lambda E, w=w, dt=dt, tb=tb, bank0=bank0: E.matmul(
                            psq[bank0 + tb][:],
                            w[:, dt * 128:(dt + 1) * 128],
                            xT[:, dt * T + tb * 512: dt * T + (tb + 1) * 512],
                            start=(dt == 0), stop=(dt == ND - 1)),
                            inc=(mm if (dt == ND - 1 and tb == 3) else None))
                mmd[pt] = mm.n
                pr = pre[pt % 2]
                ac = acc[pt % 2]
                qb = qbuf[pt % 2]
                p.wait("vector", mm, mmd[pt])
                if pt >= 2:
                    p.wait("vector", aa)
                    p.wait("vector", hw, qst.get(pt - 2, 0))
                for tb in range(4):
                    p.on("vector", lambda E, pr=pr, tb=tb, bank0=bank0: E.tensor_copy(
                        pr[:, tb * 512:(tb + 1) * 512], psq[bank0 + tb][:]), inc=vv)
                cvd[pt] = vv.n
                vs()
                wc = convw[:, pt * 4:(pt + 1) * 4]
                p.on("vector", lambda E, pr=pr, ac=ac, wc=wc: E.tensor_scalar(
                    ac[:], pr[:], wc[:, 3:4], None, op0=ALU.mult), inc=vv)
                vs()
                p.on("vector", lambda E, pr=pr, ac=ac, wc=wc: E.scalar_tensor_tensor(
                    ac[:, 1:T], pr[:, 0:T - 1], wc[:, 2:3], ac[:, 1:T],
                    op0=ALU.mult, op1=ALU.add), inc=vv)
                vs()
                p.on("vector", lambda E, pr=pr, ac=ac, wc=wc: E.scalar_tensor_tensor(
                    ac[:, 2:T], pr[:, 0:T - 2], wc[:, 1:2], ac[:, 2:T],
                    op0=ALU.mult, op1=ALU.add), inc=vv)
                vs()
                p.on("vector", lambda E, pr=pr, ac=ac, wc=wc: E.scalar_tensor_tensor(
                    ac[:, 3:T], pr[:, 0:T - 3], wc[:, 0:1], ac[:, 3:T],
                    op0=ALU.mult, op1=ALU.add), inc=vv)
                conv_v = vv.n
                p.wait("scalar", vv, conv_v)
                p.on("scalar", lambda E, ac=ac, qb=qb: E.activation(qb[:], ac[:], AF.Silu),
                     inc=aa)
                p.wait("sync", aa)
                p.dma("sync", hw, dr["qkvc"][pt, :, :], qb[:])
                qst[pt] = hw.n

        if stop_after == "qkv":
            p.wait("sync", hw)
            p.wait("gpsimd", hw)
            return

        for vh in range(2):
            with ExitStack() as s4:
                e4 = s4.enter_context
                wz = e4(nc.sbuf_tensor(f"A_wz{vh}", [128, ND * 2048], BF16))
                wzf = e4(nc.sbuf_tensor(f"A_wzf{vh}", [128, 2048], F32))
                zb = [e4(nc.sbuf_tensor(f"A_zb{vh}{i}", [128, 2048], BF16)) for i in range(2)]
                psz = [e4(nc.psum_tensor(f"A_psz{vh}_{i}", [128, 512], F32)) for i in range(4)]
                for engx in ("gpsimd", "tensor", "vector", "scalar", "sync"):
                    p.wait(engx, hw)
                    p.wait(engx, vv)
                    p.wait(engx, aa)
                    p.wait(engx, mm)
                for dt in range(ND):
                    p.dma("sync", hw, wzf[:], dr["wz"][dt, :, vh * 2048:(vh + 1) * 2048])
                    p.wait("scalar", hw)
                    p.on("scalar", lambda E, dt=dt: E.activation(
                        wz[:, dt * 2048:(dt + 1) * 2048], wzf[:], AF.Identity), inc=aa)
                    p.wait("sync", aa)
                p.wait("tensor", aa)
                zsd = {}
                for tt in range(NT):
                    if tt >= 1:
                        p.wait("tensor", aa, zsd[tt - 1])
                    for dt in range(ND):
                        for vb in range(4):
                            p.on("tensor", lambda E, tt=tt, dt=dt, vb=vb: E.matmul(
                                psz[vb][:],
                                xT[:, dt * T + tt * 128: dt * T + tt * 128 + 128],
                                wz[:, dt * 2048 + vb * 512: dt * 2048 + (vb + 1) * 512],
                                start=(dt == 0), stop=(dt == ND - 1)),
                                inc=(mm if (dt == ND - 1 and vb == 3) else None))
                    zb_ = zb[tt % 2]
                    p.wait("scalar", mm)
                    if tt >= 2:
                        p.wait("scalar", hw)
                    for vb in range(4):
                        p.on("scalar", lambda E, zb_=zb_, vb=vb: E.activation(
                            zb_[:, vb * 512:(vb + 1) * 512], psz[vb][:], AF.Silu), inc=aa)
                    zsd[tt] = aa.n
                    p.wait("sync", aa)
                    p.dma("sync", hw, dr["zs"][tt, :, vh * 2048:(vh + 1) * 2048], zb_[:])
        p.wait("sync", hw)
        p.wait("gpsimd", hw)


def build_l0aA_program(stop_after=None):
    nc = bass.Bass("TRN2", target_bir_lowering=False)
    dr = {}
    dr["x0s"] = nc.dram_tensor("x0s", [T, DM], F32, kind="ExternalInput")
    dr["wqkv"] = nc.dram_tensor("wqkv", [64, 128, DM], F32, kind="ExternalInput")
    dr["wz"] = nc.dram_tensor("wz", [16, 128, 4096], F32, kind="ExternalInput")
    dr["wba"] = nc.dram_tensor("wba", [128, 1088], F32, kind="ExternalInput")
    dr["convw"] = nc.dram_tensor("convw", [128, 256], F32, kind="ExternalInput")
    dr["ident"] = nc.dram_tensor("ident", [128, 128], BF16, kind="ExternalInput")
    dr["idf32"] = nc.dram_tensor("idf32", [128, 128], F32, kind="ExternalInput")
    dr["qkvc"] = nc.dram_tensor("qkvc", [64, 128, T], BF16, kind="ExternalOutput")
    dr["zs"] = nc.dram_tensor("zs", [16, 128, 4096], BF16, kind="ExternalOutput")
    dr["betaT"] = nc.dram_tensor("betaT", [T, 32], F32, kind="ExternalOutput")
    dr["lgT"] = nc.dram_tensor("lgT", [T, 32], F32, kind="ExternalOutput")
    dr["elT"] = nc.dram_tensor("elT", [T, 32], F32, kind="ExternalOutput")
    dr["ervT"] = nc.dram_tensor("ervT", [T, 32], F32, kind="ExternalOutput")
    dr["gamC"] = nc.dram_tensor("gamC", [16, 32], F32, kind="ExternalOutput")
    es = ExitStack()
    with es:
        p = Plan(nc, es)
        emit_l0a_A(p, dr, stop_after=stop_after)
        p.emit()
    return nc


def l0aA_host_inputs(inputs):
    Wq = np.asarray(inputs["L0_attn_W_qkvz"], np.float32)
    wqkvT = np.ascontiguousarray(Wq[0:8192].T)
    wzT = np.ascontiguousarray(Wq[8192:].T)
    wbaT = np.ascontiguousarray(np.asarray(inputs["L0_attn_W_ba"], np.float32).T)
    cw = np.asarray(inputs["L0_attn_conv_w"], np.float32).reshape(8192, 4)
    return {
        "wqkv": pretile_lhsT(wqkvT),
        "wz": np.ascontiguousarray(wzT.reshape(16, 128, 4096)),
        "wba": np.ascontiguousarray(np.concatenate([
            wbaT.reshape(16, 128, 64).transpose(1, 0, 2).reshape(128, 1024),
            np.tile(np.concatenate([
                np.asarray(inputs["L0_attn_dt_bias"], np.float32).reshape(1, 32),
                np.asarray(inputs["L0_attn_A_log"], np.float32).reshape(1, 32)], axis=1),
                (128, 1))], axis=1)),
        "convw": np.ascontiguousarray(cw.reshape(64, 128, 4).transpose(1, 0, 2).reshape(128, 256)),

        "ident": np.eye(128, dtype=BF),
        "idf32": np.eye(128, dtype=np.float32),
    }




def emit_l0a_B(p: Plan, dr):
    nc, es = p.nc, p.es
    NH2 = 32
    NP = 16

    ld = p.sem("ld")
    hw = p.sem("hw")
    mm = p.sem("mm")
    vv = p.sem("vv")
    aa = p.sem("aa")

    def vs():
        p.wait("vector", vv)

    def asy():
        p.wait("scalar", aa)

    with ExitStack() as st:
        e = st.enter_context
        idn = e(nc.sbuf_tensor("B_idn", [128, 128], BF16))
        idf = e(nc.sbuf_tensor("B_idf", [128, 128], F32))
        maskA = e(nc.sbuf_tensor("B_mA", [128, 128], F32))
        maskM = e(nc.sbuf_tensor("B_mM", [128, 128], F32))
        onesf = e(nc.sbuf_tensor("B_1f", [128, 128], F32))
        eps6 = e(nc.sbuf_tensor("B_eps6", [128, 1], F32))
        lgc = e(nc.sbuf_tensor("B_lg", [128, 512], F32))
        elc = e(nc.sbuf_tensor("B_el", [128, 512], F32))
        evc = e(nc.sbuf_tensor("B_ev", [128, 512], F32))
        btc = e(nc.sbuf_tensor("B_bt", [128, 512], F32))
        bnegc = e(nc.sbuf_tensor("B_bn", [128, 512], F32))
        S_all = e(nc.sbuf_tensor("B_S", [128, NH2 * 128], F32))
        slab = [e(nc.sbuf_tensor(f"B_sl{i}", [128, 64 * 256], BF16)) for i in range(2)]
        zbuf = [e(nc.sbuf_tensor(f"B_z{i}", [128, 4096], BF16)) for i in range(2)]
        coreb = [e(nc.sbuf_tensor(f"B_cb{i}", [128, 4096], BF16)) for i in range(2)]
        cfo = [e(nc.sbuf_tensor(f"B_cf{i}", [128, 4096], BF16)) for i in range(2)]
        kqT = e(nc.sbuf_tensor("B_kqT", [128, NP * 256], BF16))
        n2 = e(nc.sbuf_tensor("B_n2", [128, 32], F32))
        scrK = e(nc.sbuf_tensor("B_scrK", [128, 128], BF16))
        scrZ = e(nc.sbuf_tensor("B_scrZ", [128, 4096], BF16))
        khat = e(nc.sbuf_tensor("B_kh", [128, 128], BF16))
        qhat = e(nc.sbuf_tensor("B_qh", [128, 128], BF16))
        bcf = e(nc.sbuf_tensor("B_bc", [128, 128], F32))
        DA = e(nc.sbuf_tensor("B_DA", [128, 128], BF16))
        DM = e(nc.sbuf_tensor("B_DM", [128, 128], BF16))
        Aneg = e(nc.sbuf_tensor("B_An", [128, 128], BF16))
        Mt = e(nc.sbuf_tensor("B_Mt", [128, 128], BF16))
        BTs = e(nc.sbuf_tensor("B_BTs", [128, 7 * 128], BF16))
        Bcur = e(nc.sbuf_tensor("B_Bc", [128, 128], BF16))
        Rb = [e(nc.sbuf_tensor(f"B_R{i}", [128, 256], BF16)) for i in range(2)]
        Wt = e(nc.sbuf_tensor("B_Wt", [128, 128], BF16))
        Ub = e(nc.sbuf_tensor("B_Ub", [128, 128], BF16))
        O1s = e(nc.sbuf_tensor("B_O1", [128, 128], BF16))
        S0b = e(nc.sbuf_tensor("B_S0", [128, 128], BF16))
        Sg = e(nc.sbuf_tensor("B_Sg", [128, 128], F32))
        Kb = e(nc.sbuf_tensor("B_Kb", [128, 128], BF16))
        gcol = e(nc.sbuf_tensor("B_gc", [128, 4], F32))
        rmsc = e(nc.sbuf_tensor("B_rm", [128, 4], F32))

        ps = [e(nc.psum_tensor(f"B_ps{i}", [128, 512], F32)) for i in range(8)]

        lastr = {i: None for i in range(8)}

        def rd(bank, sem):
            lastr[bank] = (sem, sem.n)

        def guard(bank):
            g_ = lastr[bank]
            if g_ is not None:
                p.wait("tensor", g_[0], g_[1])

        p.dma("sync", hw, idn[:], dr["ident"][:, :])
        p.dma("sync", hw, idf[:], dr["idf32"][:, :])
        p.dma("sync", hw, maskA[:], dr["maskA"][:, :])
        p.dma("sync", hw, maskM[:], dr["maskM"][:, :])
        p.dma("sync", hw, lgc[:].rearrange("p (c h) -> p c h", c=NT),
              dr["lgT"].rearrange("(c p) h -> p c h", p=128))
        p.dma("sync", hw, elc[:].rearrange("p (c h) -> p c h", c=NT),
              dr["elT"].rearrange("(c p) h -> p c h", p=128))
        p.dma("sync", hw, evc[:].rearrange("p (c h) -> p c h", c=NT),
              dr["ervT"].rearrange("(c p) h -> p c h", p=128))
        p.dma("sync", hw, btc[:].rearrange("p (c h) -> p c h", c=NT),
              dr["betaT"].rearrange("(c p) h -> p c h", p=128))
        p.on("vector", lambda E: E.memset(onesf[:], 1.0), inc=vv)
        p.on("vector", lambda E: E.memset(eps6[:], 1e-6), inc=vv)
        p.on("vector", lambda E: E.memset(S_all[:], 0.0), inc=vv)
        p.wait("vector", hw)
        p.on("vector", lambda E: E.tensor_scalar(bnegc[:], btc[:], -1.0, None,
                                                 op0=ALU.mult), inc=vv)
        consts_v = vv.n

        def load_slab(si):
            b = slab[si % 2]
            p.dma("sync", hw, b[:].rearrange("p (ct f) -> p ct f", ct=64),
                  dr["qkvc"][:, :, si * 256:(si + 1) * 256].rearrange("ct p f -> p ct f"))

        def load_z(c):
            p.dma("sync", hw, zbuf[c % 2][:], dr["zs"][c, :, :])

        load_slab(0)
        load_z(0)
        slab_hw = {0: hw.n}
        z_hw = {0: hw.n}
        p.wait("tensor", vv, consts_v)
        p.wait("scalar", vv, consts_v)

        cf_st = {}
        slab_mm = {}
        for c in range(NT):
            si = c // 2
            co = c % 2
            sb = slab[si % 2]
            if co == 0 and si + 1 <= 7:
                if si >= 1:
                    p.wait("sync", mm, slab_mm.get(si - 1, 0))
                load_slab(si + 1)
                slab_hw[si + 1] = hw.n
            if c + 1 < NT:
                if c >= 1:
                    p.wait("sync", vv, cf_st.get(("zread", c - 1), 0))
                load_z(c + 1)
                z_hw[c + 1] = hw.n
            p.wait("tensor", hw, slab_hw[si])

            cb = coreb[c % 2]
            if c >= 2:
                p.wait("vector", vv, cf_st.get(("cf", c - 2), 0))
            if c == 0:
                prev_o_m = 0
            for hq in range(NP):
                par = hq % 2
                koff = (16 + hq) * 256 + co * 128
                qoff = hq * 256 + co * 128
                guard(0)
                p.on("tensor", lambda E, sb=sb, koff=koff: E.matmul(
                    ps[0][:, 0:128], sb[:, koff:koff + 128], idn[:],
                    start=True, stop=True))
                p.on("tensor", lambda E, sb=sb, qoff=qoff: E.matmul(
                    ps[0][:, 128:256], sb[:, qoff:qoff + 128], idn[:],
                    start=True, stop=True))
                for hh in range(2):
                    voff = (32 + 2 * hq + hh) * 256 + co * 128
                    p.on("tensor", lambda E, sb=sb, voff=voff, hh=hh: E.matmul(
                        ps[0][:, 256 + hh * 128:384 + hh * 128],
                        sb[:, voff:voff + 128], idn[:], start=True, stop=True),
                        inc=(mm if hh == 1 else None))
                n1m = mm.n
                p.wait("vector", mm, n1m)
                p.on("vector", lambda E, hq=hq: E.tensor_copy(
                    kqT[:, hq * 256: hq * 256 + 128], ps[0][:, 0:128]), inc=vv)
                p.on("vector", lambda E, hq=hq: E.tensor_copy(
                    kqT[:, hq * 256 + 128: hq * 256 + 256], ps[0][:, 128:256]), inc=vv)
                vs()
                p.on("vector", lambda E, hq=hq: E.scalar_tensor_tensor(
                    scrK[:], kqT[:, hq * 256: hq * 256 + 128], 1.0,
                    kqT[:, hq * 256: hq * 256 + 128], op0=ALU.mult, op1=ALU.mult,
                    accum_out=n2[:, 2 * hq: 2 * hq + 1]), inc=vv)
                p.on("vector", lambda E, hq=hq: E.scalar_tensor_tensor(
                    scrK[:], kqT[:, hq * 256 + 128: hq * 256 + 256], 1.0,
                    kqT[:, hq * 256 + 128: hq * 256 + 256], op0=ALU.mult, op1=ALU.mult,
                    accum_out=n2[:, 2 * hq + 1: 2 * hq + 2]), inc=vv)
                for hh in range(2):
                    h = 2 * hq + hh
                    p.on("vector", lambda E, hh=hh, h=h, c=c: E.tensor_scalar(
                        Rb[hh][:, 128:256], ps[0][:, 256 + hh * 128:384 + hh * 128],
                        btc[:, c * 32 + h: c * 32 + h + 1], None, op0=ALU.mult), inc=vv)
                rd(0, vv)
                vs()
                p.on("vector", lambda E, hq=hq: E.tensor_scalar(
                    n2[:, 2 * hq:2 * hq + 2], n2[:, 2 * hq:2 * hq + 2], 1e-6, None,
                    op0=ALU.add), inc=vv)
                p.wait("scalar", vv)
                p.on("scalar", lambda E, hq=hq: E.activation(
                    n2[:, 2 * hq:2 * hq + 2], n2[:, 2 * hq:2 * hq + 2], AF.Ln), inc=aa)
                asy()
                p.on("scalar", lambda E, hq=hq: E.activation(
                    n2[:, 2 * hq:2 * hq + 2], n2[:, 2 * hq:2 * hq + 2],
                    AF.Exp, scale=-0.5), inc=aa)
                p.wait("vector", aa)
                p.on("vector", lambda E, hq=hq: E.tensor_scalar(
                    kqT[:, hq * 256: hq * 256 + 128], kqT[:, hq * 256: hq * 256 + 128],
                    n2[:, 2 * hq: 2 * hq + 1], None, op0=ALU.mult), inc=vv)
                p.on("vector", lambda E, hq=hq: E.tensor_scalar(
                    kqT[:, hq * 256 + 128: hq * 256 + 256],
                    kqT[:, hq * 256 + 128: hq * 256 + 256],
                    n2[:, 2 * hq + 1: 2 * hq + 2], None, op0=ALU.mult), inc=vv)
                norm_v = vv.n
                guard(1)
                p.wait("tensor", vv, norm_v)
                p.on("tensor", lambda E, hq=hq: E.matmul(
                    ps[1][:, 0:128], kqT[:, hq * 256: hq * 256 + 128], idn[:],
                    start=True, stop=True))
                p.on("tensor", lambda E, hq=hq: E.matmul(
                    ps[1][:, 128:256], kqT[:, hq * 256 + 128: hq * 256 + 256], idn[:],
                    start=True, stop=True), inc=mm)
                btm = mm.n
                p.wait("vector", mm, btm)
                p.on("vector", lambda E: E.tensor_copy(khat[:], ps[1][:, 0:128]), inc=vv)
                p.on("vector", lambda E: E.tensor_copy(qhat[:], ps[1][:, 128:256]), inc=vv)
                rd(1, vv)
                kh_v = vv.n
                pmb = 2 + par
                guard(pmb)
                p.wait("tensor", vv, kh_v)
                p.on("tensor", lambda E, pmb=pmb: E.matmul(
                    ps[pmb][:, 0:128], khat[:], khat[:], start=True, stop=True))
                p.on("tensor", lambda E, pmb=pmb: E.matmul(
                    ps[pmb][:, 128:256], khat[:], qhat[:], start=True, stop=True), inc=mm)
                pm_m = mm.n

                for hh in range(2):
                    h = 2 * hq + hh
                    lcol = lgc[:, c * 32 + h: c * 32 + h + 1]
                    ecol = elc[:, c * 32 + h: c * 32 + h + 1]
                    vcol = evc[:, c * 32 + h: c * 32 + h + 1]
                    bcol = btc[:, c * 32 + h: c * 32 + h + 1]
                    bncol = bnegc[:, c * 32 + h: c * 32 + h + 1]
                    p.wait("vector", mm, prev_o_m)
                    p.on("vector", lambda E, h=h: E.tensor_copy(
                        S0b[:], S_all[:, h * 128:(h + 1) * 128]), inc=vv)
                    s0_v = vv.n
                    p.on("vector", lambda E, lcol=lcol: E.tensor_scalar(
                        bcf[:], onesf[:], lcol, None, op0=ALU.mult), inc=vv)
                    bc_v = vv.n
                    p.wait("tensor", vv, bc_v)
                    p.on("tensor", lambda E, pmb=pmb, hh=hh: E.matmul(
                        ps[pmb][:, 256 + hh * 128:384 + hh * 128], bcf[:], idf[:],
                        start=True, stop=True), inc=mm)
                    trb_m = mm.n
                    p.wait("vector", mm, trb_m)
                    p.on("vector", lambda E, pmb=pmb, hh=hh, lcol=lcol: E.scalar_tensor_tensor(
                        bcf[:], ps[pmb][:, 256 + hh * 128:384 + hh * 128], lcol, maskA[:],
                        op0=ALU.subtract, op1=ALU.add), inc=vv)
                    vs()
                    p.wait("scalar", vv)
                    p.on("scalar", lambda E: E.activation(DA[:], bcf[:], AF.Exp,
                                                          scale=-1.0), inc=aa)
                    p.wait("vector", aa)
                    p.on("vector", lambda E, pmb=pmb, hh=hh, lcol=lcol: E.scalar_tensor_tensor(
                        bcf[:], ps[pmb][:, 256 + hh * 128:384 + hh * 128], lcol, maskM[:],
                        op0=ALU.subtract, op1=ALU.add), inc=vv)
                    rd(pmb, vv)
                    vs()
                    p.wait("scalar", vv)
                    p.on("scalar", lambda E: E.activation(DM[:], bcf[:], AF.Exp), inc=aa)
                    dm_a = aa.n
                    p.wait("vector", aa, dm_a)
                    p.wait("vector", mm, pm_m)
                    p.on("vector", lambda E, pmb=pmb, bncol=bncol: E.scalar_tensor_tensor(
                        Aneg[:], DA[:], bncol, ps[pmb][:, 0:128],
                        op0=ALU.mult, op1=ALU.mult), inc=vv)
                    p.wait("vector", mm, prev_o_m)
                    p.on("vector", lambda E, pmb=pmb: E.tensor_tensor(
                        Mt[:], DM[:], ps[pmb][:, 128:256], op=ALU.mult), inc=vv)
                    rd(pmb, vv)
                    an_v = vv.n
                    p.on("vector", lambda E, hq=hq, hh=hh, ecol=ecol, bcol=bcol: E.tensor_scalar(
                        Rb[hh][:, 0:128], kqT[:, hq * 256: hq * 256 + 128],
                        ecol, bcol, op0=ALU.mult, op1=ALU.mult), inc=vv)
                    guard(1)
                    p.wait("tensor", vv, an_v)
                    p.on("tensor", lambda E: E.matmul(ps[1][:, 256:384], Aneg[:], idn[:],
                                                      start=True, stop=True), inc=mm)
                    at_m = mm.n
                    p.wait("vector", mm, at_m)
                    p.on("vector", lambda E: E.tensor_copy(BTs[:, 0:128],
                                                           ps[1][:, 256:384]), inc=vv)
                    rd(1, vv)
                    p.on("vector", lambda E: E.tensor_copy(Bcur[:], Aneg[:]), inc=vv)
                    dbl_v = vv.n
                    for lv in range(6):
                        dbb = 4 + (lv % 2)
                        guard(dbb)
                        p.wait("tensor", vv, dbl_v)
                        if lv < 5:
                            p.on("tensor", lambda E, lv=lv, dbb=dbb: E.matmul(
                                ps[dbb][:, 0:128], BTs[:, lv * 128:(lv + 1) * 128],
                                Bcur[:], start=True, stop=True))
                        p.on("tensor", lambda E, lv=lv, dbb=dbb: E.matmul(
                            ps[dbb][:, 128:256], Bcur[:],
                            BTs[:, lv * 128:(lv + 1) * 128], start=True, stop=True),
                            inc=mm)
                        db_m = mm.n
                        p.wait("vector", mm, db_m)
                        if lv < 5:
                            p.on("vector", lambda E, dbb=dbb: E.tensor_copy(
                                Bcur[:], ps[dbb][:, 0:128]), inc=vv)
                        p.on("vector", lambda E, lv=lv, dbb=dbb: E.tensor_copy(
                            BTs[:, (lv + 1) * 128:(lv + 2) * 128],
                            ps[dbb][:, 128:256]), inc=vv)
                        rd(dbb, vv)
                        dbl_v = vv.n
                    app_v = vv.n
                    for k in range(7):
                        guard(6)
                        p.wait("tensor", vv, app_v)
                        p.on("tensor", lambda E, hh=hh: E.matmul(
                            ps[6][:, 0:256], idn[:], Rb[hh][:], start=True, stop=False))
                        p.on("tensor", lambda E, k=k, hh=hh: E.matmul(
                            ps[6][:, 0:256], BTs[:, k * 128:(k + 1) * 128], Rb[hh][:],
                            start=False, stop=True), inc=mm)
                        ap_m = mm.n
                        p.wait("vector", mm, ap_m)
                        p.on("vector", lambda E, hh=hh: E.tensor_copy(
                            Rb[hh][:], ps[6][:, 0:256]), inc=vv)
                        rd(6, vv)
                        app_v = vv.n
                    guard(1)
                    p.wait("tensor", vv, app_v)
                    p.on("tensor", lambda E, hh=hh: E.matmul(
                        ps[1][:, 384:512], Rb[hh][:, 0:128], idn[:],
                        start=True, stop=True), inc=mm)
                    wt_m = mm.n
                    p.wait("vector", mm, wt_m)
                    p.on("vector", lambda E: E.tensor_copy(Wt[:], ps[1][:, 384:512]), inc=vv)
                    rd(1, vv)
                    wt_v = vv.n
                    guard(7)
                    p.wait("tensor", vv, wt_v)
                    p.wait("tensor", vv, s0_v)
                    p.on("tensor", lambda E: E.matmul(ps[7][:, 256:384], Wt[:], S0b[:],
                                                      start=True, stop=True), inc=mm)
                    ws_m = mm.n
                    p.wait("vector", mm, ws_m)
                    p.on("vector", lambda E, hh=hh: E.scalar_tensor_tensor(
                        Ub[:], ps[7][:, 256:384], -1.0, Rb[hh][:, 128:256],
                        op0=ALU.mult, op1=ALU.add), inc=vv)
                    rd(7, vv)
                    u_v = vv.n
                    guard(7)
                    p.wait("tensor", vv, u_v)
                    p.on("tensor", lambda E: E.matmul(ps[7][:, 0:128], Mt[:], Ub[:],
                                                      start=True, stop=True))
                    p.on("tensor", lambda E: E.matmul(ps[7][:, 128:256], qhat[:], S0b[:],
                                                      start=True, stop=True), inc=mm)
                    o_m = mm.n
                    prev_o_m = o_m
                    p.wait("vector", mm, o_m)
                    p.on("vector", lambda E: E.tensor_copy(O1s[:], ps[7][:, 0:128]), inc=vv)
                    vs()
                    p.on("vector", lambda E, h=h, cb=cb, ecol=ecol: E.scalar_tensor_tensor(
                        cb[:, h * 128:(h + 1) * 128], ps[7][:, 128:256], ecol, O1s[:],
                        op0=ALU.mult, op1=ALU.add), inc=vv)
                    rd(7, vv)
                    p.on("vector", lambda E, ecol=ecol, vcol=vcol: E.tensor_tensor(
                        gcol[:, 0:1], ecol, vcol, op=ALU.mult), inc=vv)
                    vs()
                    p.wait("scalar", vv)
                    p.on("scalar", lambda E, h=h: E.activation(
                        Sg[:], S_all[:, h * 128:(h + 1) * 128], AF.Identity,
                        scale=gcol[:, 0:1]), inc=aa)
                    sg_a = aa.n
                    p.on("vector", lambda E, hq=hq, vcol=vcol: E.tensor_scalar(
                        Kb[:], kqT[:, hq * 256: hq * 256 + 128], vcol, None,
                        op0=ALU.mult), inc=vv)
                    kb_v = vv.n
                    guard(7)
                    p.wait("tensor", vv, u_v)
                    p.wait("tensor", vv, kb_v)
                    p.on("tensor", lambda E: E.matmul(ps[7][:, 384:512], Kb[:], Ub[:],
                                                      start=True, stop=True), inc=mm)
                    sa_m = mm.n
                    p.wait("vector", mm, sa_m)
                    p.wait("vector", aa, sg_a)
                    p.on("vector", lambda E, h=h: E.tensor_tensor(
                        S_all[:, h * 128:(h + 1) * 128], Sg[:], ps[7][:, 384:512],
                        op=ALU.add), inc=vv)
                    rd(7, vv)
            slab_mm[si] = mm.n
            cfb = cfo[c % 2]
            p.wait("vector", hw, z_hw[c])
            p.on("vector", lambda E, cb=cb: E.scalar_tensor_tensor(
                scrZ[:], cb[:], 1.0, cb[:], op0=ALU.mult, op1=ALU.mult,
                accum_out=rmsc[:, 0:1]), inc=vv)
            vs()
            p.on("vector", lambda E: E.tensor_scalar(
                rmsc[:, 1:2], rmsc[:, 0:1], 1.0 / 4096.0, 1e-6,
                op0=ALU.mult, op1=ALU.add), inc=vv)
            p.wait("scalar", vv)
            p.on("scalar", lambda E: E.activation(rmsc[:, 2:3], rmsc[:, 1:2], AF.Ln), inc=aa)
            asy()
            p.on("scalar", lambda E: E.activation(rmsc[:, 2:3], rmsc[:, 2:3],
                                                  AF.Exp, scale=-0.5), inc=aa)
            p.wait("vector", aa)
            if c >= 2:
                p.wait("vector", hw, cf_st[("st", c - 2)])
            p.on("vector", lambda E, cb=cb, cfb=cfb, c=c: E.scalar_tensor_tensor(
                cfb[:], cb[:], rmsc[:, 2:3], zbuf[c % 2][:],
                op0=ALU.mult, op1=ALU.mult), inc=vv)
            cf_st[("zread", c)] = vv.n
            cf_st[("cf", c)] = vv.n
            p.wait("sync", vv)
            p.dma("sync", hw, dr["cfs"][c, :, :], cfb[:])
            cf_st[("st", c)] = hw.n
        p.wait("sync", hw)
        p.wait("gpsimd", hw)


def build_l0aB_program():
    nc = bass.Bass("TRN2", target_bir_lowering=False)
    dr = {}
    dr["qkvc"] = nc.dram_tensor("qkvc", [64, 128, T], BF16, kind="ExternalInput")
    dr["zs"] = nc.dram_tensor("zs", [16, 128, 4096], BF16, kind="ExternalInput")
    dr["lgT"] = nc.dram_tensor("lgT", [T, 32], F32, kind="ExternalInput")
    dr["elT"] = nc.dram_tensor("elT", [T, 32], F32, kind="ExternalInput")
    dr["ervT"] = nc.dram_tensor("ervT", [T, 32], F32, kind="ExternalInput")
    dr["betaT"] = nc.dram_tensor("betaT", [T, 32], F32, kind="ExternalInput")
    dr["ident"] = nc.dram_tensor("ident", [128, 128], BF16, kind="ExternalInput")
    dr["idf32"] = nc.dram_tensor("idf32", [128, 128], F32, kind="ExternalInput")
    dr["maskA"] = nc.dram_tensor("maskA", [128, 128], F32, kind="ExternalInput")
    dr["maskM"] = nc.dram_tensor("maskM", [128, 128], F32, kind="ExternalInput")
    dr["cfs"] = nc.dram_tensor("cfs", [16, 128, 4096], BF16, kind="ExternalOutput")
    es = ExitStack()
    with es:
        p = Plan(nc, es)
        emit_l0a_B(p, dr)
        p.emit()
    return nc


def l0aB_host_inputs():
    i_ = np.arange(128)
    maskA = np.where(i_[None, :] >= i_[:, None], 1e9, 0.0).astype(np.float32)
    maskM = np.where(i_[None, :] < i_[:, None], -1e9, 0.0).astype(np.float32)
    return {
        "ident": np.eye(128, dtype=BF),
        "idf32": np.eye(128, dtype=np.float32),
        "maskA": maskA,
        "maskM": maskM,
    }




def emit_l0a_C(p: Plan, dr):
    nc, es = p.nc, p.es
    NV = 32

    ld = p.sem("ld")
    hw = p.sem("hw")
    mm = p.sem("mm")
    vv = p.sem("vv")
    aa = p.sem("aa")

    def vs():
        p.wait("vector", vv)

    def asy():
        p.wait("scalar", aa)

    with ExitStack() as st:
        e = st.enter_context
        idn = e(nc.sbuf_tensor("C_idn", [128, 128], BF16))
        epscol = e(nc.sbuf_tensor("C_eps", [128, 1], F32))
        nw1 = e(nc.sbuf_tensor("C_nw", [128, NV], F32))
        wos = e(nc.sbuf_tensor("C_wos", [128, NV * DM], BF16))
        wosf = e(nc.sbuf_tensor("C_wosf", [128, DM], F32))
        crow = [e(nc.sbuf_tensor(f"C_cr{i}", [128, 4096], BF16)) for i in range(2)]
        aot = e(nc.sbuf_tensor("C_aot", [128, NV * 128], BF16))
        x0r = [e(nc.sbuf_tensor(f"C_x0{i}", [128, DM], F32)) for i in range(2)]
        yrow = e(nc.sbuf_tensor("C_y", [128, DM], BF16))
        x1row = e(nc.sbuf_tensor("C_x1", [128, DM], F32))
        lnscr = e(nc.sbuf_tensor("C_ls", [128, DM], BF16))
        cols = e(nc.sbuf_tensor("C_cols", [128, 8], F32))
        pst = [e(nc.psum_tensor(f"C_pt{i}", [128, 512], F32)) for i in range(2)]
        pso = [e(nc.psum_tensor(f"C_po{i}", [128, 512], F32)) for i in range(4)]

        p.dma("sync", hw, idn[:], dr["ident"][:, :])
        p.dma("sync", hw, nw1[:], dr["nw"][:, :])
        p.on("vector", lambda E: E.memset(epscol[:], 1e-5), inc=vv)
        p.wait("vector", hw)
        p.on("vector", lambda E: E.tensor_scalar(nw1[:], nw1[:], 1.0, None,
                                                 op0=ALU.add), inc=vv)
        vs()
        for vt in range(NV):
            p.dma("sync", hw, wosf[:], dr["woT"][vt, :, :])
            p.wait("scalar", hw)
            p.wait("scalar", vv, 2)
            p.on("scalar", lambda E, vt=vt: E.activation(
                wos[:, vt * DM:(vt + 1) * DM], wosf[:], AF.Identity,
                scale=nw1[:, vt:vt + 1]), inc=aa)
            p.wait("sync", aa)
        p.wait("tensor", aa)

        apl = {}
        trv = {}
        for tt in range(NT):
            cr = crow[tt % 2]
            xr = x0r[tt % 2]
            if tt >= 2:
                p.wait("sync", mm, trv[tt - 2])
            p.dma("sync", hw, cr[:], dr["cfs"][tt, :, :])
            p.dma("sync", hw, xr[:], dr["x0s"][tt * 128:(tt + 1) * 128, :])
            p.wait("tensor", hw)
            if tt >= 1:
                p.wait("tensor", vv)
            for g4 in range(8):
                bank = pst[g4 % 2]
                if tt * 8 + g4 >= 2:
                    p.wait("tensor", vv, vv.n - 1)
                for j in range(4):
                    vt = 4 * g4 + j
                    p.on("tensor", lambda E, cr=cr, vt=vt, j=j, bank=bank: E.matmul(
                        bank[:, j * 128:(j + 1) * 128],
                        cr[:, vt * 128:(vt + 1) * 128], idn[:],
                        start=True, stop=True), inc=mm)
                p.wait("vector", mm)
                p.on("vector", lambda E, g4=g4, bank=bank: E.tensor_copy(
                    aot[:, g4 * 512:(g4 + 1) * 512], bank[:, 0:512]), inc=vv)
            aot_v = vv.n
            trv[tt] = mm.n
            p.wait("tensor", vv, aot_v)
            if tt >= 1:
                p.wait("tensor", vv, apl.get(("y", tt - 1), 0))
            for vt in range(NV):
                for tb in range(4):
                    p.on("tensor", lambda E, vt=vt, tb=tb: E.matmul(
                        pso[tb][:], aot[:, vt * 128:(vt + 1) * 128],
                        wos[:, vt * DM + tb * 512: vt * DM + (tb + 1) * 512],
                        start=(vt == 0), stop=(vt == NV - 1)),
                        inc=(mm if (vt == NV - 1 and tb == 3) else None))
            om = mm.n
            trv[tt] = mm.n
            p.wait("vector", mm, om)
            if tt >= 1:
                p.wait("vector", hw, apl.get(("st", tt - 1), 0))
            for tb in range(4):
                p.on("vector", lambda E, xr=xr, tb=tb: E.scalar_tensor_tensor(
                    yrow[:, tb * 512:(tb + 1) * 512], pso[tb][:], 1.0,
                    xr[:, tb * 512:(tb + 1) * 512], op0=ALU.mult, op1=ALU.add), inc=vv)
            apl[("y", tt)] = vv.n
            p.wait("scalar", hw)
            ln_apply(p, None, "", yrow, x1row, lnscr, cols, epscol)
            p.wait("sync", aa)
            p.dma("sync", hw, dr["x1s"][tt * 128:(tt + 1) * 128, :], x1row[:])
            apl[("st", tt)] = hw.n
        p.wait("sync", hw)
        p.wait("gpsimd", hw)


def build_l0aC_program():
    nc = bass.Bass("TRN2", target_bir_lowering=False)
    dr = {}
    dr["cfs"] = nc.dram_tensor("cfs", [16, 128, 4096], BF16, kind="ExternalInput")
    dr["x0s"] = nc.dram_tensor("x0s", [T, DM], F32, kind="ExternalInput")
    dr["woT"] = nc.dram_tensor("woT", [32, 128, DM], F32, kind="ExternalInput")
    dr["nw"] = nc.dram_tensor("nw", [128, 32], F32, kind="ExternalInput")
    dr["ident"] = nc.dram_tensor("ident", [128, 128], BF16, kind="ExternalInput")
    dr["x1s"] = nc.dram_tensor("x1s", [T, DM], F32, kind="ExternalOutput")
    es = ExitStack()
    with es:
        p = Plan(nc, es)
        emit_l0a_C(p, dr)
        p.emit()
    return nc


def l0aC_host_inputs(inputs):
    woT = np.ascontiguousarray(np.asarray(inputs["L0_attn_W_out"], np.float32).T
                               .reshape(32, 128, DM))
    nw = np.ascontiguousarray(
        np.asarray(inputs["L0_attn_norm_w"], np.float32).reshape(32, 128).T.copy())
    return {"woT": woT, "nw": nw, "ident": np.eye(128, dtype=BF)}



def build_l0f_program(debug=False):
    nc = bass.Bass("TRN2", target_bir_lowering=False)
    k = dict(kind="ExternalOutput") if debug else {}
    dr = {}
    dr["x1s"] = nc.dram_tensor("x1s", [T, DM], F32, kind="ExternalInput")
    dr["wg"] = nc.dram_tensor("wg", [64, 128, DM], F32, kind="ExternalInput")
    dr["wu"] = nc.dram_tensor("wu", [64, 128, DM], F32, kind="ExternalInput")
    dr["wd"] = nc.dram_tensor("wd", [16, 128, 8192], F32, kind="ExternalInput")
    dr["ident"] = nc.dram_tensor("ident", [128, 128], BF16, kind="ExternalInput")
    dr["hs"] = nc.dram_tensor("hs", [64, 128, T], BF16, **k)
    dr["x1Ts"] = nc.dram_tensor("x1Ts", [16, 128, T], BF16, **k)
    dr["ys"] = nc.dram_tensor("ys", [16, 128, T], BF16, **k)
    dr["wdbf"] = nc.dram_tensor("wdbf", [16, 128, 8192], BF16)
    dr["x2s"] = nc.dram_tensor("x2s", [T, DM], F32, kind="ExternalOutput")
    es = ExitStack()
    with es:
        p = Plan(nc, es)
        emit_l0f(p, dr)
        p.emit()
    return nc


def l0f_host_inputs(inputs):
    wg = np.asarray(inputs["L0_ffn_w_gate"], np.float32)
    wu = np.asarray(inputs["L0_ffn_w_up"], np.float32)
    wd = np.asarray(inputs["L0_ffn_w_down"], np.float32)
    return {
        "wg": pretile_lhsT(np.ascontiguousarray(wg.T)),
        "wu": pretile_lhsT(np.ascontiguousarray(wu.T)),
        "wd": pretile_lhsT(np.ascontiguousarray(wd.T)),
        "ident": np.eye(128, dtype=BF),
    }



def kernel(**inputs):
    x = np.asarray(inputs["L0_attn_x"])
    cores = list(range(NCORE))

    hA = l0aA_host_inputs(inputs)
    hB = l0aB_host_inputs()
    hC = l0aC_host_inputs(inputs)
    hF = l0f_host_inputs(inputs)
    hA1 = l1a_host_inputs(inputs)
    hQ = l1f_host_inputs(inputs)

    ncA = build_l0aA_program()
    ims = [dict(hA, x0s=np.ascontiguousarray(x[b])) for b in range(NCORE)]
    rA = run_bass_kernel_spmd(ncA, ims, core_ids=cores).results

    ncB = build_l0aB_program()
    ims = [dict(hB, qkvc=rA[b]["qkvc"], zs=rA[b]["zs"], lgT=rA[b]["lgT"],
                elT=rA[b]["elT"], ervT=rA[b]["ervT"], betaT=rA[b]["betaT"])
           for b in range(NCORE)]
    rB = run_bass_kernel_spmd(ncB, ims, core_ids=cores).results

    ncC = build_l0aC_program()
    ims = [dict(hC, cfs=rB[b]["cfs"], x0s=np.ascontiguousarray(x[b]))
           for b in range(NCORE)]
    rC = run_bass_kernel_spmd(ncC, ims, core_ids=cores).results

    ncF = build_l0f_program()
    ims = [dict(hF, x1s=rC[b]["x1s"]) for b in range(NCORE)]
    rF = run_bass_kernel_spmd(ncF, ims, core_ids=cores).results

    ncA1 = build_l1a_program()
    ims = [dict(hA1, x2s=rF[b]["x2s"]) for b in range(NCORE)]
    rA1 = run_bass_kernel_spmd(ncA1, ims, core_ids=cores).results

    ncQ = build_l1f_program()
    ims = [dict(hQ, x3s=rA1[b]["x3s"]) for b in range(NCORE)]
    rQ = run_bass_kernel_spmd(ncQ, ims, core_ids=cores).results

    return np.stack([np.asarray(rQ[b]["x4s"], dtype=np.float32) for b in range(NCORE)])
